# version 16
# baseline (speedup 1.0000x reference)
import numpy as np
from contextlib import ExitStack

import concourse.bass as bass
import concourse.mybir as mybir
from concourse.bass import AP
from concourse.bass_utils import run_bass_kernel_spmd

F32 = mybir.dt.float32
BF16 = mybir.dt.bfloat16
AF = mybir.ActivationFunctionType
OP = mybir.AluOpType
AX = mybir.AxisListType

TC = 256
NCH = 4096 // TC
NTI = TC // 128
DEV_STAGES = 3
DEV_DEBUG = False
MAMBA_CUT = 0
DEV_NCH = None
MAMBA_SKIP_OUT = ()
DBU_ENG = "vector"

from concourse.dve_ops import DveOp, OPS as _DVE_OPS
from concourse.dve_spec import Spec, Src0, C0, C1, C2, sq as _sq, lower as _dve_lower


def _register_dve(name, spec, ref):
    from concourse import dve_ops as _dvo
    have = {o.name for o in _DVE_OPS}
    if name in have:
        return next(o for o in _DVE_OPS if o.name == name)
    shas = {}
    for ver in ("v3", "v4"):
        try:
            shas[ver] = _dve_lower(spec, ver=ver).sha(ver)
        except Exception:
            pass
    op = DveOp(name, spec, False, shas, reference=ref)
    _DVE_OPS.append(op)
    _dvo._SUB_OPCODE_FOR_NAME[name] = _dvo._CUSTOM_DVE_ROW_BASE + len(_DVE_OPS) - 1
    _dvo.CUSTOM_DVE_SPECS[name] = spec
    _dvo.CUSTOM_DVE_REFS[name] = ref
    return op


def _qc_ref(in0, in1, s0, s1, imm2):
    sqv = in0 * in0
    c = (sqv >= s0).astype(np.float32) + 2.0 * (sqv >= s1).astype(np.float32)
    return in0 + (c * imm2 + imm2)


def _qd_ref(in0, in1, s0, s1, imm2):
    c = (in0 >= s0).astype(np.float32) + 2.0 * (in0 >= s1).astype(np.float32)
    return in0 - (c * imm2 + imm2)


_s_ = _sq(Src0)
ANT_QC = _register_dve(
    "ANT_QC", Spec(body=Src0 + (((_s_ >= C0) + ((_s_ >= C1) + (_s_ >= C1))) * C2 + C2)),
    _qc_ref)
ANT_QD = _register_dve(
    "ANT_QD", Spec(body=Src0 - (((Src0 >= C0) + ((Src0 >= C1) + (Src0 >= C1))) * C2 + C2)),
    _qd_ref)

ENGS = ("sync", "scalar", "vector", "tensor", "gpsimd")
ESEM = {"scalar": "s", "vector": "v", "tensor": "t", "gpsimd": "g", "sync": "y0"}
SEMS = ["s", "v", "t", "g", "y0", "qw", "qx", "qo", "qp",
        "qg0", "qg1", "qu0", "qu1", "qd0", "qd1", "qi0", "qi1",
        "qw0", "qw1", "qpa", "qpb", "qy"]


class Pr:

    def __init__(self):
        self.q = {e: [] for e in ENGS}
        self.cnt = {s: 0 for s in SEMS}
        self.lastw = {}

    def emit(self, eng, fn, waits=(), inc=None, k=1):
        ws = []
        for w in waits:
            if w is None:
                continue
            s, v = w
            if v is None or v <= 0:
                continue
            key = (eng, s)
            if self.lastw.get(key, 0) >= v:
                continue
            self.lastw[key] = v
            ws.append((s, v))
        if inc is not None:
            self.cnt[inc] += k
        self.q[eng].append((tuple(ws), fn, inc, k))
        return (inc, self.cnt[inc]) if inc else None

    DMAS = ("qw", "qx", "qo", "qp", "qg0", "qg1", "qu0", "qu1", "qd0", "qd1",
            "qi0", "qi1", "qw0", "qw1", "qpa", "qpb", "qy")

    def join(self, extra_sems=()):
        for e in ENGS:
            self.emit(e, lambda h: h.nop(), inc=ESEM[e])
        c = dict(self.cnt)
        allw = [(ESEM[e], c[ESEM[e]]) for e in ENGS]
        allw += [(s, c[s]) for s in self.DMAS]
        for e in ENGS:
            self.emit(e, lambda h: h.nop(), waits=allw)

    def prune_incs(self):
        import os
        comp = set(os.environ.get("PRUNE_SEMS", "s,v,t,g,y0").split(","))
        waited = {s: set() for s in self.cnt}
        for e in ENGS:
            for ws, _, _, _ in self.q[e]:
                for s, v in ws:
                    waited[s].add(v)
        rank = {s: {v: i + 1 for i, v in enumerate(sorted(waited[s]))} for s in comp}
        stats = [0, 0]
        for e in ENGS:
            s_e = ESEM[e]
            c = 0
            nq = []
            for ws, fn, inc, k in self.q[e]:
                nws = tuple((s, rank[s][v]) if s in comp else (s, v) for s, v in ws)
                if inc == s_e and s_e in comp:
                    c += k
                    stats[1] += 1
                    if c in waited[s_e]:
                        stats[0] += 1
                        nq.append((nws, fn, inc, 1))
                    else:
                        nq.append((nws, fn, None, 0))
                else:
                    nq.append((nws, fn, inc, k))
            self.q[e] = nq
        return stats


def build_nc():
    nc = bass.Bass("TRN2", target_bir_lowering=False)
    pr = Pr()

    x_d = nc.dram_tensor("x", [4096, 1024], F32, kind="ExternalInput").ap()
    y_d = nc.dram_tensor("y", [4096, 1024], F32, kind="ExternalOutput").ap()
    wip_d = nc.dram_tensor("wip", [128, 8, 4096], F32, kind="ExternalInput").ap()
    wo_d = nc.dram_tensor("wo", [128, 16, 1024], F32, kind="ExternalInput").ap()
    xpw_d = nc.dram_tensor("xpw", [128, 16, 128], F32, kind="ExternalInput").ap()
    dtw_d = nc.dram_tensor("dtw", [64, 2048], F32, kind="ExternalInput").ap()
    wgt_d = nc.dram_tensor("wgt", [128, 22, 1024], F32, kind="ExternalInput").ap()
    wut_d = nc.dram_tensor("wut", [128, 22, 1024], F32, kind="ExternalInput").ap()
    wdt_d = nc.dram_tensor("wdt", [128, 8, 2816], F32, kind="ExternalInput").ap()
    convw_d = nc.dram_tensor("convw", [128, 16, 4], F32, kind="ExternalInput").ap()
    convb_d = nc.dram_tensor("convb", [128, 16], F32, kind="ExternalInput").ap()
    alog_d = nc.dram_tensor("alog", [128, 16, 16], F32, kind="ExternalInput").ap()
    dp_d = nc.dram_tensor("dp", [128, 16], F32, kind="ExternalInput").ap()
    dtb_d = nc.dram_tensor("dtb", [128, 16], F32, kind="ExternalInput").ap()
    gam_d = nc.dram_tensor("gam", [128, 1024], F32, kind="ExternalInput").ap()
    bet_d = nc.dram_tensor("bet", [128, 1024], F32, kind="ExternalInput").ap()
    wip_s = nc.dram_tensor("wip_s", [128, 8, 4096], BF16, kind="Internal").ap()
    wo_s = nc.dram_tensor("wo_s", [128, 16, 1024], BF16, kind="Internal").ap()
    wgq_s = nc.dram_tensor("wgq_s", [128, 8, 2816], BF16, kind="Internal").ap()
    wuq_s = nc.dram_tensor("wuq_s", [128, 8, 2816], BF16, kind="Internal").ap()
    wdq_s = nc.dram_tensor("wdq_s", [128, 22, 1024], BF16, kind="Internal").ap()

    st = ExitStack()
    sb = lambda n, sh, dt: st.enter_context(nc.sbuf_tensor(n, sh, dt))
    ps = lambda n, sh, dt: st.enter_context(nc.psum_tensor(n, sh, dt))

    with st:
        sem = {n: st.enter_context(nc.semaphore(n)) for n in SEMS}

        convw = sb("convw_t", [128, 16, 4], F32)
        convb = sb("convb_t", [128, 16], F32)
        A_t = sb("A_t", [128, 256], F32)
        D_t = sb("D_t", [128, 16], F32)
        dtb = sb("dtb_t", [128, 16], F32)
        gam = sb("gam_t", [128, 1024], BF16)
        bet = sb("bet_t", [128, 1024], BF16)
        ident = sb("ident_t", [128, 128], BF16)
        ones1 = sb("ones1_t", [1, 128], BF16)
        hcar = sb("hcar_t", [128, 256], F32)
        xpw = sb("xpw_t", [128, 16, 128], BF16)
        dtw = sb("dtw_t", [128, 2048], BF16)
        stat = sb("stat_t", [128, NTI, 4], F32)
        bnb = sb("bnb_t", [128, 24], F32)
        bmv = sb("bmv_t", [128, 88], F32)
        sv = sb("sv_t", [128, 88], F32)
        iv = sb("iv_t", [128, 88], F32)
        A1 = sb("A1_t", [128, 2048], F32)
        A2 = sb("A2_t", [128, 1408], F32)
        A3 = sb("A3_t", [128, 1408], F32)
        sel = sb("sel_t", [48, 2048], BF16)
        maskc = sb("maskc_t", [128, 1], mybir.dt.int32)

        pmm = [ps("pmm%d" % i, [128, 512], F32) for i in range(4)]
        pt_b = [ps("ptb%d" % i, [128, 8, 128], BF16) for i in range(2)]
        prd = {}

        def fence_ps(key):
            t = prd.get(key)
            return [t] if t else []

        def E(eng, fn, waits=(), k=1):
            return pr.emit(eng, fn, waits, ESEM[eng], k)

        def dma(out, in_, waits=(), qsem="qw", eng="sync"):
            pr.emit(eng, lambda h, o=out, i=in_: h.dma_start(o, i), waits, qsem, 16)
            return (qsem, pr.cnt[qsem])

        def tt(out, a, b, op=OP.mult, waits=(), eng="vector"):
            return E(eng, lambda h, o=out, x=a, yy=b, p=op: h.tensor_tensor(o, x, yy, p), waits)

        def ts(out, in0, s1, s2=None, op0=OP.mult, op1=OP.bypass, waits=(), eng="vector"):
            return E(eng, lambda h, o=out, i=in0, a=s1, b=s2, p0=op0, p1=op1:
                     h.tensor_scalar(o, i, a, b, p0, p1), waits)

        def stt(out, in0, sc, in1, op0=OP.mult, op1=OP.add, waits=()):
            return E("vector", lambda h, o=out, i=in0, s_=sc, j=in1, p0=op0, p1=op1:
                     h.scalar_tensor_tensor(o, i, s_, j, p0, p1), waits)

        def act(out, in_, func, bias=0.0, scale=1.0, waits=()):
            return E("scalar", lambda h, o=out, i=in_, f=func, b=bias, sc=scale:
                     h.activation(o, i, f, bias=b, scale=sc), waits)

        def scopy(out, in_, waits=()):
            return E("scalar", lambda h, o=out, i=in_: h.copy(o, i), waits)

        def vcopy(out, in_, waits=()):
            return E("vector", lambda h, o=out, i=in_: h.tensor_copy(o, i), waits)

        def mm(out, lhsT, rhs, start, stop, waits=()):
            return E("tensor", lambda h, o=out, l=lhsT, r=rhs, s0=start, s1=stop:
                     h.matmul(o, l, r, start=s0, stop=s1), waits)

        def trans(out, in_, waits=()):
            return E("tensor", lambda h, o=out, i=in_: h.transpose(o, i, ident[:, :]), waits)

        def scan_op(out, d0, d1, init, waits=()):
            return E("vector", lambda h, o=out, a=d0, b=d1, ii=init:
                     h.tensor_tensor_scan(o, a, b, ii, OP.mult, OP.add), waits)

        def red(out, in_, op=OP.add, absval=False, waits=()):
            return E("vector", lambda h, o=out, i=in_, p=op, av=absval:
                     h.tensor_reduce(o, i, AX.X, p, apply_absolute_value=av), waits)

        def recip(out, in_, waits=()):
            return E("vector", lambda h, o=out, i=in_: h.reciprocal(o, i), waits)

        def gmemset(ap, val, waits=()):
            return E("gpsimd", lambda h, a=ap, v=val: h.memset(a, v), waits)

        def bcast16(iv_, b=16):
            return AP(iv_.tensor, iv_.offset, list(iv_.ap) + [[0, b]])

        qstate = {"gtok": None}

        def quant(in_v, out_v, fd, w_in, cbuf):
            nb = fd // 16
            cbv = cbuf[:, 0:fd]
            in3 = in_v.rearrange("p (a b) -> p a b", b=16)
            red(bmv[:, 0:nb], in3, OP.max, True, waits=w_in)
            ts(sv[:, 0:nb], bmv[:, 0:nb], 1.0 / 6.0, 1e-12, OP.mult, OP.max)
            recip(iv[:, 0:nb], sv[:, 0:nb])
            xnv = A2[:, 0:fd]
            t3 = tt(xnv.rearrange("p (a b) -> p a b", b=16), in3, bcast16(iv[:, 0:nb]))
            sqv = A3[:, 0:fd]
            ts(sqv, xnv, 1.0 / 2.25, None, OP.mult, waits=[t3])
            E("vector", lambda h, o=cbv.bitcast(mybir.dt.int32),
              i=sqv.bitcast(mybir.dt.int32), m=maskc[:, 0:1]:
              h.tensor_scalar(o, i, m, None, OP.bitwise_and))
            ts(cbv, cbv, 0.5, 12582912.0, OP.max, OP.mult)
            tt(sqv, xnv, cbv, OP.add)
            tt(sqv, sqv, cbv, OP.subtract)
            return tt(out_v.rearrange("p (a b) -> p a b", b=16),
                      sqv.rearrange("p (a b) -> p a b", b=16), bcast16(sv[:, 0:nb]))

        dma(A1[:, 0:256], alog_d.rearrange("p a b -> p (a b)"), qsem="qpa")
        dma(convw[:, :, :], convw_d, qsem="qpa")
        dma(convb[:, :], convb_d, qsem="qpa")
        dma(D_t[:, :], dp_d, qsem="qpa")
        dma(dtb[:, :], dtb_d, qsem="qpa")
        dma(A2[:, 0:1024], gam_d, qsem="qpa")
        t_bt = dma(A3[:, 0:1024], bet_d, qsem="qpa")
        t_al = t_gm = t_bt
        te = act(A_t[:, :], A1[:, 0:256], AF.Exp, waits=[t_al])
        ts(A_t[:, :], A_t[:, :], -1.0, None, OP.mult, waits=[te])
        vcopy(gam[:, :], A2[:, 0:1024], waits=[t_gm])
        vcopy(bet[:, :], A3[:, 0:1024], waits=[t_bt])
        gmemset(ident[:, :], 1.0)
        E("gpsimd", lambda h: h.affine_select(out=ident[:, :], in_=ident[:, :],
                                              pattern=[[-1, 128]], compare_op=OP.is_equal,
                                              fill=0.0, base=0, channel_multiplier=1))
        gmemset(ones1[0:1, :], 1.0)
        gmemset(maskc[:, :], 0x7f800000)
        gmemset(hcar[:, :], 0.0)
        t1 = dma(A1[:, 0:2048], xpw_d.rearrange("p a b -> p (a b)"), qsem="qpb",
                 waits=[("s", pr.cnt["s"]), ("v", pr.cnt["v"]), t_al])
        vcopy(xpw[:, :, :].rearrange("p a b -> p (a b)"), A1[:, 0:2048], waits=[t1])
        t2 = dma(A1[64:128, 0:2048], dtw_d, qsem="qpb", waits=[("v", pr.cnt["v"])])
        vcopy(dtw[64:128, :], A1[64:128, 0:2048], waits=[t2])
        gmemset(sel[:, :], 1.0)
        for pb0 in (0, 32):
            E("gpsimd", lambda h, p0=pb0: h.affine_select(
                out=sel[p0:p0 + 16, :].rearrange("p (a b) -> p a b", a=16),
                in_=sel[p0:p0 + 16, :].rearrange("p (a b) -> p a b", a=16),
                pattern=[[1, 16], [0, 128]],
                compare_op=OP.is_equal, fill=0.0, base=0,
                channel_multiplier=-1))

        with (nc.sbuf_tensor("pbig", [128, 8192], F32) as pbig,
              nc.sbuf_tensor("pbig_b", [128, 8192], BF16) as pbig_b,
              nc.sbuf_tensor("wT", [128, 22528], BF16) as wT):
            for g2 in range(4):
                tg = dma(pbig[:, :], wip_d[:, 2 * g2:2 * g2 + 2, :].rearrange("p a b -> p (a b)"),
                         waits=[("v", pr.cnt["v"])], qsem="qpb")
                tc = vcopy(pbig_b[:, :], pbig[:, :], waits=[tg])
                dma(wip_s[:, 2 * g2:2 * g2 + 2, :].rearrange("p a b -> p (a b)"),
                    pbig_b[:, :], waits=[tc], qsem="qp")
            for g2 in range(2):
                tg = dma(pbig[:, :], wo_d[:, 8 * g2:8 * g2 + 8, :].rearrange("p a b -> p (a b)"),
                         waits=[("v", pr.cnt["v"]), ("qp", pr.cnt["qp"])], qsem="qpb")
                tc = vcopy(pbig_b[:, :], pbig[:, :], waits=[tg])
                dma(wo_s[:, 8 * g2:8 * g2 + 8, :].rearrange("p a b -> p (a b)"),
                    pbig_b[:, :], waits=[tc], qsem="qp")
            for (wsrc, wdst) in ((wgt_d, wgq_s), (wut_d, wuq_s)):
                for ic in range(22):
                    tg = dma(pbig[:, 0:1024], wsrc[:, ic, :],
                             waits=[("v", pr.cnt["v"]), ("t", pr.cnt["t"]),
                                    ("qp", pr.cnt["qp"])], qsem="qpb")
                    tq = quant(pbig[:, 0:1024], pbig_b[:, 0:1024], 1024, [tg], cbuf=A1)
                    pb = pt_b[ic % 2]
                    wts = fence_ps(("pt", ic % 2)) + [tq]
                    for kc in range(8):
                        trans(pb[:, kc, :], pbig_b[:, kc * 128:(kc + 1) * 128],
                              waits=wts)
                        wts = []
                    ev = scopy(wT.rearrange("p (a b) -> p a b", a=8)[:, :, ic * 128:(ic + 1) * 128],
                               pb[:, :, :], waits=[("t", pr.cnt["t"])])
                    prd[("pt", ic % 2)] = ev
                dma(wdst[:, :, :].rearrange("p a b -> p (a b)"), wT[:, :],
                    waits=[("s", pr.cnt["s"])], qsem="qp")
            for hc in range(8):
                tg = dma(pbig[:, 0:2816], wdt_d[:, hc, :],
                         waits=[("v", pr.cnt["v"]), ("t", pr.cnt["t"]),
                                ("qp", pr.cnt["qp"])], qsem="qpb")
                tq0 = quant(pbig[:, 0:1408], pbig_b[:, 0:1408], 1408, [tg], cbuf=A1)
                tq = quant(pbig[:, 1408:2816], pbig_b[:, 1408:2816], 1408, [tq0], cbuf=A1)
                for ic in range(22):
                    pb = pt_b[ic % 2]
                    wts = fence_ps(("pt", ic % 2)) + [tq]
                    trans(pb[:, 0, :], pbig_b[:, ic * 128:(ic + 1) * 128], waits=wts)
                    ev = scopy(wT.rearrange("p (a b) -> p a b", a=22)[:, ic, hc * 128:(hc + 1) * 128],
                               pb[:, 0, :], waits=[("t", pr.cnt["t"])])
                    prd[("pt", ic % 2)] = ev
            dma(wdq_s[:, :, :].rearrange("p a b -> p (a b)"), wT[:, :],
                waits=[("s", pr.cnt["s"])], qsem="qp")
            pr.join()

        xres = sb("xres_t", [128, NTI, 1024], F32)
        xq = sb("xq_t", [128, NTI, 1024], BF16)
        xqT = sb("xqT_t", [128, 8, NTI * 128], BF16)
        h_f = sb("h_f", [128, NTI, 2816], F32)
        hq_ti2 = sb("hq_ti", [128, NTI, 2816], BF16)
        hqT_all = sb("hqT_t", [128, 22, NTI * 128], BF16)
        sgb = sb("sgb_t", [128, 512], BF16)
        lnm = sb("lnm_t", [128, NTI, 1024], BF16)
        lnmT = sb("lnmT_t", [128, 8, TC], BF16)
        xcv = sb("xcv_t", [128, 16, TC + 3], BF16)
        zsl = sb("zsl_t", [128, 16, TC], BF16)
        u16 = sb("u16_t", [128, 16, TC], BF16)
        hbuf = sb("hbuf_t", [128, 16, TC], BF16)
        yg = sb("yg_t", [128, 16, TC], BF16)
        Bbc = sb("Bbc_t", [128, 16, TC], BF16)
        Cbc = sb("Cbc_t", [128, 16, TC], BF16)
        dtr = sb("dtr_t", [128, TC], BF16)
        bcsb = sb("bcsb_t", [48, 2 * TC], BF16)
        delta2 = sb("delta2_t", [128, 2, TC], F32)
        du2 = sb("du2_t", [128, 2, TC], BF16)
        accA = sb("accA_t", [128, TC], F32)
        accB = sb("accB_t", [128, TC], F32)
        yv = sb("yv_t", [128, TC], F32)
        y2v = sb("y2v_t", [128, TC], F32)
        wreg = sb("wreg_t", [128, 16384], BF16)

        gmemset(xcv[:, :, :].rearrange("p a b -> p (a b)"), 0.0)
        pr.join()

        wranges = []

        def wfence(off, size):
            w = [t for (o, sz, t) in wranges if o < off + size and off < o + sz]
            return w

        def wmark(off, size, tok):
            wranges[:] = [(o, sz, t) for (o, sz, t) in wranges
                          if not (o >= off and o + sz <= off + size)]
            wranges.append((off, size, tok))

        def wview(off, n_a, n_b):
            return wreg[:, off:off + n_a * n_b].rearrange("p (a b) -> p a b", a=n_a)

        def ln_stats_group(src, w_in):
            w = list(w_in)
            for ti in range(NTI):
                E("vector", lambda h, i=src[:, ti, 0:512], o=bnb[:, ti * 12:ti * 12 + 6]:
                  h.bn_stats(o, i), waits=w + [("s", pr.cnt["s"])])
                w = []
                tl = E("vector", lambda h, i=src[:, ti, 512:1024],
                       o=bnb[:, ti * 12 + 6:ti * 12 + 12]: h.bn_stats(o, i))
            pr.emit("vector", lambda h: h.nop(), waits=[tl])
            for ti in range(NTI):
                ta = E("vector", lambda h, o=stat[:, ti, 0:2], i=bnb[:, ti * 12:ti * 12 + 12]:
                       h.bn_aggr(o, i))
            pr.emit("vector", lambda h: h.nop(), waits=[ta])
            vvar = stat[:, :, 1]
            vr = stat[:, :, 2]
            tv = ts(vr, vvar, 1e-5, None, OP.add)
            tl = act(vr, vr, AF.Ln, waits=[tv])
            return act(vr, vr, AF.Exp, scale=-0.5, waits=[tl])

        def ln_apply(ti, xin, out_v, w=()):
            return ts(out_v, xin, stat[:, ti, 0:1], stat[:, ti, 2:3],
                      OP.subtract, OP.mult, waits=w)

        I_GROUPS = [(0, 512), (512, 512), (1024, 512), (1536, 512),
                    (2048, 512), (2560, 256)]
        H_SL = [(i * 256, 256) for i in range(4)]

        def ffn_stage(tx):
            tst = ln_stats_group(xres, [tx])
            for ti in range(NTI):
                ta = ln_apply(ti, xres[:, ti, :], A1[:, 0:1024], [tst])
                quant(A1[:, 0:1024], xq[:, ti, :], 1024, [ta], cbuf=h_f[:, ti, :])
            for ti in range(NTI):
                pb = pt_b[ti % 2]
                wts = fence_ps(("pt", ti % 2)) + [("v", pr.cnt["v"])]
                for dc in range(8):
                    trans(pb[:, dc, :], xq[:, ti, dc * 128:(dc + 1) * 128], waits=wts)
                    wts = []
                ev = scopy(xqT[:, :, ti * 128:(ti + 1) * 128], pb[:, :, :],
                           waits=[("t", pr.cnt["t"])])
                prd[("pt", ti % 2)] = ev
            txqT = ("s", pr.cnt["s"])
            gtok = {}
            for gi in range(2):
                i0, iw = I_GROUPS[gi]
                go, uo = (gi % 2) * 4096, 8192 + (gi % 2) * 4096
                gtok[("g", gi)] = (go, dma(wview(go, 8, iw), wgq_s[:, :, i0:i0 + iw],
                                           waits=wfence(go, 8 * iw),
                                           qsem="qg%d" % (gi % 2)))
                gtok[("u", gi)] = (uo, dma(wview(uo, 8, iw), wuq_s[:, :, i0:i0 + iw],
                                           waits=wfence(uo, 8 * iw),
                                           qsem="qu%d" % (gi % 2)))
            thp = {"tok": None}

            def gateup_group(gi):
                i0, iw = I_GROUPS[gi]
                go, twg = gtok[("g", gi)]
                uo, twu = gtok[("u", gi)]
                for ti in range(NTI):
                    pg, pu = pmm[0], pmm[1]
                    wts = fence_ps(("pmm", 0)) + [twg, txqT]
                    for dc in range(8):
                        mm(pg[:, 0:iw], xqT[:, dc, ti * 128:(ti + 1) * 128],
                           wview(go, 8, iw)[:, dc, :], dc == 0, dc == 7, waits=wts)
                        wts = []
                    tg = ("t", pr.cnt["t"])
                    wts = fence_ps(("pmm", 1)) + [twu]
                    for dc in range(8):
                        mm(pu[:, 0:iw], xqT[:, dc, ti * 128:(ti + 1) * 128],
                           wview(uo, 8, iw)[:, dc, :], dc == 0, dc == 7, waits=wts)
                        wts = []
                    tu = ("t", pr.cnt["t"])
                    tsg = act(sgb[:, 0:iw], pg[:, 0:iw], AF.Silu,
                              waits=[tg, thp["tok"]])
                    prd[("pmm", 0)] = tsg
                    th = tt(h_f[:, ti, i0:i0 + iw], sgb[:, 0:iw],
                            pu[:, 0:iw], OP.mult, waits=[tu, tsg])
                    prd[("pmm", 1)] = th
                    thp["tok"] = th
                tmk = ("t", pr.cnt["t"])
                wmark(go, 8 * iw, tmk)
                wmark(uo, 8 * iw, tmk)
                if gi + 2 < len(I_GROUPS):
                    i2, iw2 = I_GROUPS[gi + 2]
                    go2, uo2 = (gi % 2) * 4096, 8192 + (gi % 2) * 4096
                    gtok[("g", gi + 2)] = (go2, dma(wview(go2, 8, iw2),
                                                    wgq_s[:, :, i2:i2 + iw2],
                                                    waits=wfence(go2, 8 * iw2),
                                                    qsem="qg%d" % (gi % 2)))
                    gtok[("u", gi + 2)] = (uo2, dma(wview(uo2, 8, iw2),
                                                    wuq_s[:, :, i2:i2 + iw2],
                                                    waits=wfence(uo2, 8 * iw2),
                                                    qsem="qu%d" % (gi % 2)))
            for gi in (0, 1, 2):
                gateup_group(gi)
            tq_half = {}
            for ti in range(NTI):
                tq_half[(ti, 0)] = quant(h_f[:, ti, 0:1408], hq_ti2[:, ti, 0:1408],
                                         1408, [], cbuf=A1)
            for gi in (3, 4, 5):
                gateup_group(gi)
            for ti in range(NTI):
                tq_half[(ti, 1)] = quant(h_f[:, ti, 1408:2816],
                                         hq_ti2[:, ti, 1408:2816], 1408, [], cbuf=A1)
            for ti in range(NTI):
                tqh = tq_half[(ti, 1)]
                for g3 in range(3):
                    ics = list(range(8 * g3, min(22, 8 * g3 + 8)))
                    pb = pt_b[g3 % 2]
                    wts = fence_ps(("pt", g3 % 2)) + [tq_half[(ti, 0)], tqh]
                    for j, ic in enumerate(ics):
                        trans(pb[:, j, :], hq_ti2[:, ti, ic * 128:(ic + 1) * 128],
                              waits=wts)
                        wts = []
                    ev = scopy(hqT_all[:, ics[0]:ics[-1] + 1, ti * 128:(ti + 1) * 128],
                               pb[:, 0:len(ics), :], waits=[("t", pr.cnt["t"])])
                    prd[("pt", g3 % 2)] = ev
            tT = ("s", pr.cnt["s"])
            wd_tok = {}
            for si in range(2):
                o0, ow = H_SL[si]
                off = (si % 2) * 5632
                wd_tok[si] = dma(wview(off, 22, ow), wdq_s[:, :, o0:o0 + ow],
                                 waits=wfence(off, 22 * ow), qsem="qd%d" % (si % 2))
            for si, (o0, ow) in enumerate(H_SL):
                off = (si % 2) * 5632
                twd = wd_tok[si]
                for ti in range(NTI):
                    pd = pmm[2 + si % 2]
                    wts = fence_ps(("pmm", 2 + si % 2)) + [twd, tT]
                    for ic in range(22):
                        mm(pd[:, 0:ow], hqT_all[:, ic, ti * 128:(ti + 1) * 128],
                           wview(off, 22, ow)[:, ic, :], ic == 0, ic == 21,
                           waits=wts)
                        wts = []
                    tre = stt(xres[:, ti, o0:o0 + ow], pd[:, 0:ow], 0.5,
                              xres[:, ti, o0:o0 + ow], OP.mult, OP.add,
                              waits=[("t", pr.cnt["t"])])
                    prd[("pmm", 2 + si % 2)] = tre
                wmark(off, 22 * ow, ("t", pr.cnt["t"]))
                if si + 2 < len(H_SL):
                    o2, ow2 = H_SL[si + 2]
                    wd_tok[si + 2] = dma(wview(off, 22, ow2),
                                         wdq_s[:, :, o2:o2 + ow2],
                                         waits=wfence(off, 22 * ow2),
                                         qsem="qd%d" % (si % 2))

        def mamba_stage(tx, skip_out=False):
            scr = h_f
            tst = ln_stats_group(xres, [tx])
            for ti in range(NTI):
                ln_apply(ti, xres[:, ti, :], scr[:, ti, 0:1024], [tst])
            tst2 = ln_stats_group(scr[:, :, 0:1024], [("v", pr.cnt["v"])])
            for ti in range(NTI):
                tb = ln_apply(ti, scr[:, ti, 0:1024], A2[:, 0:1024], [tst2])
                tc = tt(A3[:, 0:1024], A2[:, 0:1024], gam[:, :], OP.mult, waits=[tb])
                tt(lnm[:, ti, :], A3[:, 0:1024], bet[:, :], OP.add, waits=[tc])
            for ti in range(NTI):
                pb = pt_b[ti % 2]
                wts = fence_ps(("pt", ti % 2)) + [("v", pr.cnt["v"])]
                for kc in range(8):
                    trans(pb[:, kc, :], lnm[:, ti, kc * 128:(kc + 1) * 128], waits=wts)
                    wts = []
                ev = scopy(lnmT[:, :, ti * 128:(ti + 1) * 128], pb[:, :, :],
                           waits=[("t", pr.cnt["t"])])
                prd[("pt", ti % 2)] = ev
            if MAMBA_CUT == 1:
                return
            tT = ("s", pr.cnt["s"])
            wip_tok = {}
            for sl in range(2):
                off = (sl % 2) * 8192
                wip_tok[sl] = dma(wview(off, 8, 1024),
                                  wip_s[:, :, sl * 1024:(sl + 1) * 1024],
                                  waits=wfence(off, 8192), qsem="qi%d" % (sl % 2))
            for sl in range(4):
                off = (sl % 2) * 8192
                twip = wip_tok[sl]
                for mc8 in range(8):
                    m = sl * 8 + mc8
                    pd = pmm[m % 2]
                    wts = fence_ps(("pmm", m % 2)) + [twip, tT]
                    for kc in range(8):
                        mm(pd[:, 0:TC], wview(off, 8, 1024)[:, kc, mc8 * 128:(mc8 + 1) * 128],
                           lnmT[:, kc, :], kc == 0, kc == 7, waits=wts)
                        wts = []
                    tmv = ("t", pr.cnt["t"])
                    if m < 16:
                        ev = scopy(xcv[:, m, 3:TC + 3], pd[:, 0:TC], waits=[tmv])
                    else:
                        ev = act(zsl[:, m - 16, :], pd[:, 0:TC], AF.Silu, waits=[tmv])
                    prd[("pmm", m % 2)] = ev
                wmark(off, 8192, ("t", pr.cnt["t"]))
                if sl + 2 < 4:
                    wip_tok[sl + 2] = dma(wview(off, 8, 1024),
                                          wip_s[:, :, (sl + 2) * 1024:(sl + 3) * 1024],
                                          waits=wfence(off, 8192),
                                          qsem="qi%d" % (sl % 2))
            tz = ("s", pr.cnt["s"])
            for dc in range(16):
                w_in = [tz] if dc == 0 else []
                ts(accA[:, :], xcv[:, dc, 0:TC], convw[:, dc, 0:1], None,
                   OP.mult, OP.bypass, waits=w_in)
                stt(accB[:, :], xcv[:, dc, 1:TC + 1], convw[:, dc, 1:2], accA[:, :])
                stt(accA[:, :], xcv[:, dc, 2:TC + 2], convw[:, dc, 2:3], accB[:, :])
                t3 = stt(accB[:, :], xcv[:, dc, 3:TC + 3], convw[:, dc, 3:4], accA[:, :])
                act(u16[:, dc, :], accB[:, :], AF.Silu, bias=convb[:, dc:dc + 1],
                    waits=[t3])
                vcopy(xcv[:, dc, 0:3], xcv[:, dc, TC:TC + 3])
            if MAMBA_CUT == 2:
                return
            tu_all = ("s", pr.cnt["s"])
            pxp = pmm[2]
            wts = fence_ps(("pmm", 2)) + [tu_all]
            for kc in range(16):
                mm(pxp[:, 0:TC], xpw[:, kc, :], u16[:, kc, :], kc == 0, kc == 15,
                   waits=wts)
                wts = []
            txp = ("t", pr.cnt["t"])
            e1 = scopy(dtr[64:128, :], pxp[64:128, 0:TC], waits=[txp])
            scopy(bcsb[0:16, 0:TC], pxp[0:16, 0:TC])
            e3 = scopy(bcsb[32:48, 0:TC], pxp[32:48, 0:TC])
            prd[("pmm", 2)] = e3
            for n in range(16):
                for which in range(2):
                    bk = 2 + ((n * 2 + which) % 2)
                    pb2 = pmm[bk]
                    wts = fence_ps(("pmm", bk)) + [e3]
                    p0 = 0 if which == 0 else 32
                    mm(pb2[:, 0:TC], sel[p0:p0 + 16, n * 128:(n + 1) * 128],
                       bcsb[p0:p0 + 16, 0:TC], True, True, waits=wts)
                    dst = Bbc if which == 0 else Cbc
                    ev = scopy(dst[:, n, :], pb2[:, 0:TC], waits=[("t", pr.cnt["t"])])
                    prd[("pmm", bk)] = ev
            if MAMBA_CUT == 3:
                return
            tbc = ("s", pr.cnt["s"])
            par_tok = {0: [], 1: []}
            scan_tok = {k: None for k in range(8)}
            for dc in range(16):
                db = dc % 2
                pd = pmm[dc % 4]
                wts = fence_ps(("pmm", dc % 4)) + ([e1] if dc == 0 else []) + par_tok[db]
                mm(pd[:, 0:TC], dtw[64:128, dc * 128:(dc + 1) * 128], dtr[64:128, :],
                   True, True, waits=wts)
                tdt = ("t", pr.cnt["t"])
                dl = delta2[:, db, :]
                duv = du2[:, db, :]
                tm0 = ts(dl, pd[:, 0:TC], dtb[:, dc:dc + 1], 40.0, OP.add, OP.min,
                         waits=[tdt] + par_tok[db])
                tm1 = act(dl, dl, AF.Exp, waits=[tm0])
                tm2 = act(dl, dl, AF.Ln, bias=1.0, waits=[tm1])
                tm3 = stt(dl, pd[:, 0:TC], dtb[:, dc:dc + 1], dl, OP.add, OP.max,
                          waits=[tm2])
                prd[("pmm", dc % 4)] = tm3
                tdu = tt(duv, dl, u16[:, dc, :], OP.mult, waits=[tm3])
                dball = h_f[:, 0, 0:2048].bitcast(BF16).rearrange(
                    "p (a b) -> p a b", a=16)
                dA8 = h_f[:, 1, 0:2048].rearrange("p (a b) -> p a b", a=8)
                duv_b = AP(duv.tensor, duv.offset,
                           [list(duv.ap[0]), [0, 16], list(duv.ap[1])])
                tt(dball, duv_b, Bbc[:, :, :], OP.mult, waits=[tdu, tbc])
                for n in range(16):
                    nb = n % 8
                    tda = act(dA8[:, nb, :], dl, AF.Exp,
                              scale=A_t[:, dc * 16 + n:dc * 16 + n + 1],
                              waits=[scan_tok[nb], tm3])
                    tsc = scan_op(hbuf[:, n, :], dA8[:, nb, :], dball[:, n, :],
                                  hcar[:, dc * 16 + n:dc * 16 + n + 1],
                                  waits=[tda])
                    scan_tok[nb] = tsc
                tcar = vcopy(hcar[:, dc * 16:dc * 16 + 16],
                             hbuf[:, :, TC - 1:TC].rearrange("p a b -> p (a b)"))
                tpr = tt(hbuf[:, :, :].rearrange("p a b -> p (a b)"),
                         hbuf[:, :, :].rearrange("p a b -> p (a b)"),
                         Cbc[:, :, :].rearrange("p a b -> p (a b)"), OP.mult,
                         waits=[tcar])
                tyr = red(yv[:, :], hbuf[:, :, :].transpose([0, 2, 1]), OP.add,
                          waits=[tpr])
                for _k in range(8):
                    scan_tok[_k] = tyr
                ty2 = stt(y2v[:, :], u16[:, dc, :], D_t[:, dc:dc + 1], yv[:, :],
                          OP.mult, OP.add, waits=[tyr])
                tt(yg[:, dc, :], y2v[:, :], zsl[:, dc, :], OP.mult, waits=[ty2])
                par_tok[db] = [("s", pr.cnt["s"]), ("v", pr.cnt["v"]),
                               ("g", pr.cnt["g"])]
            if MAMBA_CUT == 4 or skip_out:
                return
            tyg_all = ("v", pr.cnt["v"])
            wo_tok = {}
            for nc_i in range(2):
                off = nc_i * 8192
                wo_tok[nc_i] = dma(wview(off, 16, 512),
                                   wo_s[:, :, nc_i * 512:(nc_i + 1) * 512],
                                   waits=wfence(off, 8192), qsem="qw%d" % (nc_i % 2))
            for nc_i in range(2):
                off = nc_i * 8192
                two = wo_tok[nc_i]
                for ti in range(NTI):
                    pd = pmm[ti % 2]
                    wts = fence_ps(("pmm", ti % 2)) + [two, tyg_all]
                    for kc in range(16):
                        mm(pd[:, :], yg[:, kc, ti * 128:(ti + 1) * 128],
                           wview(off, 16, 512)[:, kc, :], kc == 0, kc == 15, waits=wts)
                        wts = []
                    tre = tt(xres[:, ti, nc_i * 512:(nc_i + 1) * 512],
                             xres[:, ti, nc_i * 512:(nc_i + 1) * 512], pd[:, :],
                             OP.add, waits=[("t", pr.cnt["t"])])
                    prd[("pmm", ti % 2)] = tre
                wmark(off, 8192, ("t", pr.cnt["t"]))

        dbg = {}
        if DEV_DEBUG:
            dbg["xq"] = nc.dram_tensor("dbg_xq", [128, NTI, 1024], BF16,
                                       kind="ExternalOutput").ap()
            dbg["hf"] = nc.dram_tensor("dbg_hf", [128, NTI, 2816], F32,
                                       kind="ExternalOutput").ap()
            dbg["wg"] = nc.dram_tensor("dbg_wg", [128, 8, 2816], BF16,
                                       kind="ExternalOutput").ap()
            dbg["xqT"] = nc.dram_tensor("dbg_xqT", [128, 8, NTI * 128], BF16,
                                        kind="ExternalOutput").ap()
            dbg["ln"] = nc.dram_tensor("dbg_ln", [128, 1024], F32,
                                       kind="ExternalOutput").ap()
            dbg["stat"] = nc.dram_tensor("dbg_stat", [128, NTI, 4], F32,
                                         kind="ExternalOutput").ap()
            dbg["sv"] = nc.dram_tensor("dbg_sv", [128, 88], F32,
                                       kind="ExternalOutput").ap()
            dbg["Bbc"] = nc.dram_tensor("dbg_Bbc", [128, 16, TC], BF16,
                                        kind="ExternalOutput").ap()
            dbg["Cbc"] = nc.dram_tensor("dbg_Cbc", [128, 16, TC], BF16,
                                        kind="ExternalOutput").ap()
            dbg["u16"] = nc.dram_tensor("dbg_u16", [128, 16, TC], BF16,
                                        kind="ExternalOutput").ap()
            dbg["zsl"] = nc.dram_tensor("dbg_zsl", [128, 16, TC], BF16,
                                        kind="ExternalOutput").ap()
            dbg["dtr"] = nc.dram_tensor("dbg_dtr", [128, TC], BF16,
                                        kind="ExternalOutput").ap()
            dbg["yg"] = nc.dram_tensor("dbg_yg", [128, 16, TC], BF16,
                                       kind="ExternalOutput").ap()
            dbg["hb"] = nc.dram_tensor("dbg_hb", [128, 16, TC], BF16,
                                       kind="ExternalOutput").ap()
        for c in range(DEV_NCH or NCH):
            tx = dma(xres[:, :, :],
                     x_d[c * TC:(c + 1) * TC, :].rearrange("(a p) d -> p a d", p=128),
                     qsem="qx", waits=[("qy", pr.cnt["qy"]), ("v", pr.cnt["v"])])
            if DEV_STAGES >= 1 and DEV_STAGES != 5:
                ffn_stage(tx)
            else:
                E("vector", lambda h: h.nop(), waits=[tx])
            if DEV_DEBUG and c == 0:
                w_all = [(s_, pr.cnt[s_]) for s_ in ("v", "s", "t")]
                dma(dbg["xq"], xq[:, :, :], waits=w_all, qsem="qo")
                dma(dbg["ln"], A1[:, 0:1024], waits=w_all, qsem="qo")
                dma(dbg["stat"], stat[:, :, :], waits=w_all, qsem="qo")
                dma(dbg["sv"], sv[:, :], waits=w_all, qsem="qo")
                dma(dbg["hf"], h_f[:, :, :], waits=w_all, qsem="qo")
                dma(dbg["xqT"], xqT[:, :, :], waits=w_all, qsem="qo")
                tb = dma(wview(0, 8, 1024), wgq_s[:, :, 0:1024], waits=fence_ps(("wr", 0)))
                dma(dbg["wg"][:, :, 0:1024], wview(0, 8, 1024), waits=[tb], qsem="qo")
                prd[("wr", 0)] = ("t", pr.cnt["t"])
            if DEV_STAGES >= 2 or DEV_STAGES == 5:
                mamba_stage(tx, skip_out=(c in MAMBA_SKIP_OUT))
                if DEV_DEBUG and c == 0:
                    w_all = [(s_, pr.cnt[s_]) for s_ in ("v", "s", "t", "g")]
                    for k_, b_ in (("Bbc", Bbc), ("Cbc", Cbc), ("u16", u16),
                                   ("zsl", zsl), ("yg", yg), ("hb", hbuf)):
                        if k_ in dbg:
                            dma(dbg[k_], b_[:, :, :], waits=w_all, qsem="qo")
                    dma(dbg["dtr"], dtr[:, :], waits=w_all, qsem="qo")
            if DEV_STAGES >= 3:
                ffn_stage(tx)
            dma(y_d[c * TC:(c + 1) * TC, :].rearrange("(a p) d -> p a d", p=128),
                xres[:, :, :],
                waits=[("v", pr.cnt["v"]), ("s", pr.cnt["s"]), ("t", pr.cnt["t"])],
                qsem="qy")
        pr.emit("sync", lambda h: h.nop(), waits=[("qy", pr.cnt["qy"])])
        with nc.Block() as block:
            def runner(eng):
                def go(h):
                    for ws, fn, inc, k in pr.q[eng]:
                        for s_, v_ in ws:
                            h.wait_ge(sem[s_], v_)
                        op = fn(h)
                        if inc is not None:
                            op.then_inc(sem[inc], k)
                return go
            block.sync(runner("sync"))
            block.scalar(runner("scalar"))
            block.vector(runner("vector"))
            block.tensor(runner("tensor"))
            block.gpsimd(runner("gpsimd"))
    return nc


def _host_layout(inp):
    f = np.float32
    g = lambda k: np.asarray(inp[k], dtype=f)
    d = {}
    d["wip"] = np.ascontiguousarray(g("L0_attn_in_proj_w").reshape(8, 128, 4096).transpose(1, 0, 2))
    d["wo"] = np.ascontiguousarray(g("L0_attn_out_proj_w").reshape(16, 128, 1024).transpose(1, 0, 2))
    xpw_r = g("L0_attn_x_proj_w")
    z16 = np.zeros((2048, 16), np.float32)
    xpw_r = np.concatenate([xpw_r[:, 64:80], z16, xpw_r[:, 80:96], z16,
                            xpw_r[:, 0:64]], axis=1)
    d["xpw"] = np.ascontiguousarray(xpw_r.reshape(16, 128, 128).transpose(1, 0, 2))
    d["dtw"] = np.ascontiguousarray(g("L0_attn_dt_proj_w"))
    d["wgt"] = np.ascontiguousarray(g("L0_ffn_w_gate").reshape(22, 128, 1024).transpose(1, 0, 2))
    d["wut"] = np.ascontiguousarray(g("L0_ffn_w_up").reshape(22, 128, 1024).transpose(1, 0, 2))
    d["wdt"] = np.ascontiguousarray(g("L0_ffn_w_down").reshape(8, 128, 2816).transpose(1, 0, 2))
    d["convw"] = np.ascontiguousarray(g("L0_attn_conv_w").reshape(16, 128, 4).transpose(1, 0, 2))
    d["convb"] = np.ascontiguousarray(g("L0_attn_conv_b").reshape(16, 128).T)
    d["alog"] = np.ascontiguousarray(g("L0_attn_A_log").reshape(16, 128, 16).transpose(1, 0, 2))
    d["dp"] = np.ascontiguousarray(g("L0_attn_D_param").reshape(16, 128).T)
    d["dtb"] = np.ascontiguousarray(g("L0_attn_dt_proj_b").reshape(16, 128).T)
    d["gam"] = np.ascontiguousarray(np.broadcast_to(g("L0_attn_norm_w"), (128, 1024)))
    d["bet"] = np.ascontiguousarray(np.broadcast_to(g("L0_attn_norm_b"), (128, 1024)))
    return d


def kernel(**inputs):
    hs = np.ascontiguousarray(np.asarray(inputs["L0_attn_hidden_states"], dtype=np.float32))
    shared = _host_layout(inputs)
    nc = build_nc()
    in_maps = [dict(shared, x=np.ascontiguousarray(hs[c])) for c in range(8)]
    res = run_bass_kernel_spmd(nc, in_maps, core_ids=list(range(8)))
    out = np.stack([np.asarray(res.results[c]["y"]) for c in range(8)], axis=0)
    return out.astype(np.float32)


# version 17 (final)
# speedup vs baseline: 1.0075x; 1.0075x over previous
import numpy as np
from contextlib import ExitStack

import concourse.bass as bass
import concourse.mybir as mybir
from concourse.bass import AP
from concourse.bass_utils import run_bass_kernel_spmd

F32 = mybir.dt.float32
BF16 = mybir.dt.bfloat16
AF = mybir.ActivationFunctionType
OP = mybir.AluOpType
AX = mybir.AxisListType

TC = 256
NCH = 4096 // TC
NTI = TC // 128
DEV_STAGES = 3
DEV_DEBUG = False
MAMBA_CUT = 0
DEV_NCH = None
MAMBA_SKIP_OUT = ()
DBU_ENG = "vector"

from concourse.dve_ops import DveOp, OPS as _DVE_OPS
from concourse.dve_spec import Spec, Src0, C0, C1, C2, sq as _sq, lower as _dve_lower


def _register_dve(name, spec, ref):
    from concourse import dve_ops as _dvo
    have = {o.name for o in _DVE_OPS}
    if name in have:
        return next(o for o in _DVE_OPS if o.name == name)
    shas = {}
    for ver in ("v3", "v4"):
        try:
            shas[ver] = _dve_lower(spec, ver=ver).sha(ver)
        except Exception:
            pass
    op = DveOp(name, spec, False, shas, reference=ref)
    _DVE_OPS.append(op)
    _dvo._SUB_OPCODE_FOR_NAME[name] = _dvo._CUSTOM_DVE_ROW_BASE + len(_DVE_OPS) - 1
    _dvo.CUSTOM_DVE_SPECS[name] = spec
    _dvo.CUSTOM_DVE_REFS[name] = ref
    return op


def _qc_ref(in0, in1, s0, s1, imm2):
    sqv = in0 * in0
    c = (sqv >= s0).astype(np.float32) + 2.0 * (sqv >= s1).astype(np.float32)
    return in0 + (c * imm2 + imm2)


def _qd_ref(in0, in1, s0, s1, imm2):
    c = (in0 >= s0).astype(np.float32) + 2.0 * (in0 >= s1).astype(np.float32)
    return in0 - (c * imm2 + imm2)


_s_ = _sq(Src0)
ANT_QC = _register_dve(
    "ANT_QC", Spec(body=Src0 + (((_s_ >= C0) + ((_s_ >= C1) + (_s_ >= C1))) * C2 + C2)),
    _qc_ref)
ANT_QD = _register_dve(
    "ANT_QD", Spec(body=Src0 - (((Src0 >= C0) + ((Src0 >= C1) + (Src0 >= C1))) * C2 + C2)),
    _qd_ref)

ENGS = ("sync", "scalar", "vector", "tensor", "gpsimd")
ESEM = {"scalar": "s", "vector": "v", "tensor": "t", "gpsimd": "g", "sync": "y0"}
SEMS = ["s", "v", "t", "g", "y0", "qw", "qx", "qo", "qp",
        "qg0", "qg1", "qu0", "qu1", "qd0", "qd1", "qi0", "qi1",
        "qw0", "qw1", "qpa", "qpb", "qy"]


class Pr:

    def __init__(self):
        self.q = {e: [] for e in ENGS}
        self.cnt = {s: 0 for s in SEMS}
        self.lastw = {}

    def emit(self, eng, fn, waits=(), inc=None, k=1):
        ws = []
        for w in waits:
            if w is None:
                continue
            s, v = w
            if v is None or v <= 0:
                continue
            key = (eng, s)
            if self.lastw.get(key, 0) >= v:
                continue
            self.lastw[key] = v
            ws.append((s, v))
        if inc is not None:
            self.cnt[inc] += k
        self.q[eng].append((tuple(ws), fn, inc, k))
        return (inc, self.cnt[inc]) if inc else None

    DMAS = ("qw", "qx", "qo", "qp", "qg0", "qg1", "qu0", "qu1", "qd0", "qd1",
            "qi0", "qi1", "qw0", "qw1", "qpa", "qpb", "qy")

    def join(self, extra_sems=()):
        for e in ENGS:
            self.emit(e, lambda h: h.nop(), inc=ESEM[e])
        c = dict(self.cnt)
        allw = [(ESEM[e], c[ESEM[e]]) for e in ENGS]
        allw += [(s, c[s]) for s in self.DMAS]
        for e in ENGS:
            self.emit(e, lambda h: h.nop(), waits=allw)

    def prune_incs(self):
        import os
        comp = set(os.environ.get("PRUNE_SEMS", "s,v,t,g,y0").split(","))
        waited = {s: set() for s in self.cnt}
        for e in ENGS:
            for ws, _, _, _ in self.q[e]:
                for s, v in ws:
                    waited[s].add(v)
        rank = {s: {v: i + 1 for i, v in enumerate(sorted(waited[s]))} for s in comp}
        stats = [0, 0]
        for e in ENGS:
            s_e = ESEM[e]
            c = 0
            nq = []
            for ws, fn, inc, k in self.q[e]:
                nws = tuple((s, rank[s][v]) if s in comp else (s, v) for s, v in ws)
                if inc == s_e and s_e in comp:
                    c += k
                    stats[1] += 1
                    if c in waited[s_e]:
                        stats[0] += 1
                        nq.append((nws, fn, inc, 1))
                    else:
                        nq.append((nws, fn, None, 0))
                else:
                    nq.append((nws, fn, inc, k))
            self.q[e] = nq
        return stats


def build_nc():
    nc = bass.Bass("TRN2", target_bir_lowering=False)
    pr = Pr()

    x_d = nc.dram_tensor("x", [4096, 1024], F32, kind="ExternalInput").ap()
    y_d = nc.dram_tensor("y", [4096, 1024], F32, kind="ExternalOutput").ap()
    wip_d = nc.dram_tensor("wip", [128, 8, 4096], F32, kind="ExternalInput").ap()
    wo_d = nc.dram_tensor("wo", [128, 16, 1024], F32, kind="ExternalInput").ap()
    xpw_d = nc.dram_tensor("xpw", [128, 16, 128], F32, kind="ExternalInput").ap()
    dtw_d = nc.dram_tensor("dtw", [64, 2048], F32, kind="ExternalInput").ap()
    wgt_d = nc.dram_tensor("wgt", [128, 22, 1024], F32, kind="ExternalInput").ap()
    wut_d = nc.dram_tensor("wut", [128, 22, 1024], F32, kind="ExternalInput").ap()
    wdt_d = nc.dram_tensor("wdt", [128, 8, 2816], F32, kind="ExternalInput").ap()
    convw_d = nc.dram_tensor("convw", [128, 16, 4], F32, kind="ExternalInput").ap()
    convb_d = nc.dram_tensor("convb", [128, 16], F32, kind="ExternalInput").ap()
    alog_d = nc.dram_tensor("alog", [128, 16, 16], F32, kind="ExternalInput").ap()
    dp_d = nc.dram_tensor("dp", [128, 16], F32, kind="ExternalInput").ap()
    dtb_d = nc.dram_tensor("dtb", [128, 16], F32, kind="ExternalInput").ap()
    gam_d = nc.dram_tensor("gam", [128, 1024], F32, kind="ExternalInput").ap()
    bet_d = nc.dram_tensor("bet", [128, 1024], F32, kind="ExternalInput").ap()
    wip_s = nc.dram_tensor("wip_s", [128, 8, 4096], BF16, kind="Internal").ap()
    wo_s = nc.dram_tensor("wo_s", [128, 16, 1024], BF16, kind="Internal").ap()
    wgq_s = nc.dram_tensor("wgq_s", [128, 8, 2816], BF16, kind="Internal").ap()
    wuq_s = nc.dram_tensor("wuq_s", [128, 8, 2816], BF16, kind="Internal").ap()
    wdq_s = nc.dram_tensor("wdq_s", [128, 22, 1024], BF16, kind="Internal").ap()

    st = ExitStack()
    sb = lambda n, sh, dt: st.enter_context(nc.sbuf_tensor(n, sh, dt))
    ps = lambda n, sh, dt: st.enter_context(nc.psum_tensor(n, sh, dt))

    with st:
        sem = {n: st.enter_context(nc.semaphore(n)) for n in SEMS}

        convw = sb("convw_t", [128, 16, 4], F32)
        convb = sb("convb_t", [128, 16], F32)
        A_t = sb("A_t", [128, 256], F32)
        D_t = sb("D_t", [128, 16], F32)
        dtb = sb("dtb_t", [128, 16], F32)
        gam = sb("gam_t", [128, 1024], BF16)
        bet = sb("bet_t", [128, 1024], BF16)
        ident = sb("ident_t", [128, 128], BF16)
        ones1 = sb("ones1_t", [1, 128], BF16)
        hcar = sb("hcar_t", [128, 256], F32)
        xpw = sb("xpw_t", [128, 16, 128], BF16)
        dtw = sb("dtw_t", [128, 2048], BF16)
        stat = sb("stat_t", [128, NTI, 4], F32)
        bnb = sb("bnb_t", [128, 24], F32)
        bmv = sb("bmv_t", [128, 88], F32)
        sv = sb("sv_t", [128, 88], F32)
        iv = sb("iv_t", [128, 88], F32)
        A1 = sb("A1_t", [128, 2048], F32)
        A2 = sb("A2_t", [128, 1408], F32)
        A3 = sb("A3_t", [128, 1408], F32)
        sel = sb("sel_t", [48, 2048], BF16)
        maskc = sb("maskc_t", [128, 1], mybir.dt.int32)

        pmm = [ps("pmm%d" % i, [128, 512], F32) for i in range(4)]
        pt_b = [ps("ptb%d" % i, [128, 8, 128], BF16) for i in range(2)]
        prd = {}

        def fence_ps(key):
            t = prd.get(key)
            return [t] if t else []

        def E(eng, fn, waits=(), k=1):
            return pr.emit(eng, fn, waits, ESEM[eng], k)

        def dma(out, in_, waits=(), qsem="qw", eng="sync"):
            pr.emit(eng, lambda h, o=out, i=in_: h.dma_start(o, i), waits, qsem, 16)
            return (qsem, pr.cnt[qsem])

        def tt(out, a, b, op=OP.mult, waits=(), eng="vector"):
            return E(eng, lambda h, o=out, x=a, yy=b, p=op: h.tensor_tensor(o, x, yy, p), waits)

        def ts(out, in0, s1, s2=None, op0=OP.mult, op1=OP.bypass, waits=(), eng="vector"):
            return E(eng, lambda h, o=out, i=in0, a=s1, b=s2, p0=op0, p1=op1:
                     h.tensor_scalar(o, i, a, b, p0, p1), waits)

        def stt(out, in0, sc, in1, op0=OP.mult, op1=OP.add, waits=()):
            return E("vector", lambda h, o=out, i=in0, s_=sc, j=in1, p0=op0, p1=op1:
                     h.scalar_tensor_tensor(o, i, s_, j, p0, p1), waits)

        def act(out, in_, func, bias=0.0, scale=1.0, waits=()):
            return E("scalar", lambda h, o=out, i=in_, f=func, b=bias, sc=scale:
                     h.activation(o, i, f, bias=b, scale=sc), waits)

        def scopy(out, in_, waits=()):
            return E("scalar", lambda h, o=out, i=in_: h.copy(o, i), waits)

        def vcopy(out, in_, waits=()):
            return E("vector", lambda h, o=out, i=in_: h.tensor_copy(o, i), waits)

        def mm(out, lhsT, rhs, start, stop, waits=()):
            return E("tensor", lambda h, o=out, l=lhsT, r=rhs, s0=start, s1=stop:
                     h.matmul(o, l, r, start=s0, stop=s1), waits)

        def trans(out, in_, waits=()):
            return E("tensor", lambda h, o=out, i=in_: h.transpose(o, i, ident[:, :]), waits)

        def scan_op(out, d0, d1, init, waits=()):
            return E("vector", lambda h, o=out, a=d0, b=d1, ii=init:
                     h.tensor_tensor_scan(o, a, b, ii, OP.mult, OP.add), waits)

        def red(out, in_, op=OP.add, absval=False, waits=()):
            return E("vector", lambda h, o=out, i=in_, p=op, av=absval:
                     h.tensor_reduce(o, i, AX.X, p, apply_absolute_value=av), waits)

        def recip(out, in_, waits=()):
            return E("vector", lambda h, o=out, i=in_: h.reciprocal(o, i), waits)

        def gmemset(ap, val, waits=()):
            return E("gpsimd", lambda h, a=ap, v=val: h.memset(a, v), waits)

        def bcast16(iv_, b=16):
            return AP(iv_.tensor, iv_.offset, list(iv_.ap) + [[0, b]])

        qstate = {"gtok": None}

        def quant(in_v, out_v, fd, w_in, cbuf):
            nb = fd // 16
            cbv = cbuf[:, 0:fd]
            in3 = in_v.rearrange("p (a b) -> p a b", b=16)
            red(bmv[:, 0:nb], in3, OP.max, True, waits=w_in)
            ts(sv[:, 0:nb], bmv[:, 0:nb], 1.0 / 6.0, 1e-12, OP.mult, OP.max)
            recip(iv[:, 0:nb], sv[:, 0:nb])
            xnv = A2[:, 0:fd]
            t3 = tt(xnv.rearrange("p (a b) -> p a b", b=16), in3, bcast16(iv[:, 0:nb]))
            sqv = A3[:, 0:fd]
            ts(sqv, xnv, 1.0 / 2.25, None, OP.mult, waits=[t3])
            E("vector", lambda h, o=cbv.bitcast(mybir.dt.int32),
              i=sqv.bitcast(mybir.dt.int32), m=maskc[:, 0:1]:
              h.tensor_scalar(o, i, m, None, OP.bitwise_and))
            ts(cbv, cbv, 0.5, 12582912.0, OP.max, OP.mult)
            tt(sqv, xnv, cbv, OP.add)
            tt(sqv, sqv, cbv, OP.subtract)
            return tt(out_v.rearrange("p (a b) -> p a b", b=16),
                      sqv.rearrange("p (a b) -> p a b", b=16), bcast16(sv[:, 0:nb]))

        dma(A1[:, 0:256], alog_d.rearrange("p a b -> p (a b)"), qsem="qpa")
        dma(convw[:, :, :], convw_d, qsem="qpa")
        dma(convb[:, :], convb_d, qsem="qpa")
        dma(D_t[:, :], dp_d, qsem="qpa")
        dma(dtb[:, :], dtb_d, qsem="qpa")
        dma(A2[:, 0:1024], gam_d, qsem="qpa")
        t_bt = dma(A3[:, 0:1024], bet_d, qsem="qpa")
        t_al = t_gm = t_bt
        te = act(A_t[:, :], A1[:, 0:256], AF.Exp, waits=[t_al])
        ts(A_t[:, :], A_t[:, :], -1.0, None, OP.mult, waits=[te])
        vcopy(gam[:, :], A2[:, 0:1024], waits=[t_gm])
        vcopy(bet[:, :], A3[:, 0:1024], waits=[t_bt])
        gmemset(ident[:, :], 1.0)
        E("gpsimd", lambda h: h.affine_select(out=ident[:, :], in_=ident[:, :],
                                              pattern=[[-1, 128]], compare_op=OP.is_equal,
                                              fill=0.0, base=0, channel_multiplier=1))
        gmemset(ones1[0:1, :], 1.0)
        gmemset(maskc[:, :], 0x7f800000)
        gmemset(hcar[:, :], 0.0)
        t1 = dma(A1[:, 0:2048], xpw_d.rearrange("p a b -> p (a b)"), qsem="qpb",
                 waits=[("s", pr.cnt["s"]), ("v", pr.cnt["v"]), t_al])
        vcopy(xpw[:, :, :].rearrange("p a b -> p (a b)"), A1[:, 0:2048], waits=[t1])
        t2 = dma(A1[64:128, 0:2048], dtw_d, qsem="qpb", waits=[("v", pr.cnt["v"])])
        vcopy(dtw[64:128, :], A1[64:128, 0:2048], waits=[t2])
        gmemset(sel[:, :], 1.0)
        for pb0 in (0, 32):
            E("gpsimd", lambda h, p0=pb0: h.affine_select(
                out=sel[p0:p0 + 16, :].rearrange("p (a b) -> p a b", a=16),
                in_=sel[p0:p0 + 16, :].rearrange("p (a b) -> p a b", a=16),
                pattern=[[1, 16], [0, 128]],
                compare_op=OP.is_equal, fill=0.0, base=0,
                channel_multiplier=-1))

        with (nc.sbuf_tensor("pbig", [128, 8192], F32) as pbig,
              nc.sbuf_tensor("pbig_b", [128, 8192], BF16) as pbig_b,
              nc.sbuf_tensor("wT", [128, 22528], BF16) as wT):
            for g2 in range(4):
                tg = dma(pbig[:, :], wip_d[:, 2 * g2:2 * g2 + 2, :].rearrange("p a b -> p (a b)"),
                         waits=[("v", pr.cnt["v"])], qsem="qpb")
                tc = vcopy(pbig_b[:, :], pbig[:, :], waits=[tg])
                dma(wip_s[:, 2 * g2:2 * g2 + 2, :].rearrange("p a b -> p (a b)"),
                    pbig_b[:, :], waits=[tc], qsem="qp")
            for g2 in range(2):
                tg = dma(pbig[:, :], wo_d[:, 8 * g2:8 * g2 + 8, :].rearrange("p a b -> p (a b)"),
                         waits=[("v", pr.cnt["v"]), ("qp", pr.cnt["qp"])], qsem="qpb")
                tc = vcopy(pbig_b[:, :], pbig[:, :], waits=[tg])
                dma(wo_s[:, 8 * g2:8 * g2 + 8, :].rearrange("p a b -> p (a b)"),
                    pbig_b[:, :], waits=[tc], qsem="qp")
            for (wsrc, wdst) in ((wgt_d, wgq_s), (wut_d, wuq_s)):
                for ic in range(22):
                    tg = dma(pbig[:, 0:1024], wsrc[:, ic, :],
                             waits=[("v", pr.cnt["v"]), ("t", pr.cnt["t"]),
                                    ("qp", pr.cnt["qp"])], qsem="qpb")
                    tq = quant(pbig[:, 0:1024], pbig_b[:, 0:1024], 1024, [tg], cbuf=A1)
                    pb = pt_b[ic % 2]
                    wts = fence_ps(("pt", ic % 2)) + [tq]
                    for kc in range(8):
                        trans(pb[:, kc, :], pbig_b[:, kc * 128:(kc + 1) * 128],
                              waits=wts)
                        wts = []
                    ev = scopy(wT.rearrange("p (a b) -> p a b", a=8)[:, :, ic * 128:(ic + 1) * 128],
                               pb[:, :, :], waits=[("t", pr.cnt["t"])])
                    prd[("pt", ic % 2)] = ev
                dma(wdst[:, :, :].rearrange("p a b -> p (a b)"), wT[:, :],
                    waits=[("s", pr.cnt["s"])], qsem="qp")
            for hc in range(8):
                tg = dma(pbig[:, 0:2816], wdt_d[:, hc, :],
                         waits=[("v", pr.cnt["v"]), ("t", pr.cnt["t"]),
                                ("qp", pr.cnt["qp"])], qsem="qpb")
                tq0 = quant(pbig[:, 0:1408], pbig_b[:, 0:1408], 1408, [tg], cbuf=A1)
                tq = quant(pbig[:, 1408:2816], pbig_b[:, 1408:2816], 1408, [tq0], cbuf=A1)
                for ic in range(22):
                    pb = pt_b[ic % 2]
                    wts = fence_ps(("pt", ic % 2)) + [tq]
                    trans(pb[:, 0, :], pbig_b[:, ic * 128:(ic + 1) * 128], waits=wts)
                    ev = scopy(wT.rearrange("p (a b) -> p a b", a=22)[:, ic, hc * 128:(hc + 1) * 128],
                               pb[:, 0, :], waits=[("t", pr.cnt["t"])])
                    prd[("pt", ic % 2)] = ev
            dma(wdq_s[:, :, :].rearrange("p a b -> p (a b)"), wT[:, :],
                waits=[("s", pr.cnt["s"])], qsem="qp")
            pr.join()

        xres = sb("xres_t", [128, NTI, 1024], F32)
        xq = sb("xq_t", [128, NTI, 1024], BF16)
        xqT = sb("xqT_t", [128, 8, NTI * 128], BF16)
        h_f = sb("h_f", [128, NTI, 2816], F32)
        hq_ti2 = sb("hq_ti", [128, NTI, 2816], BF16)
        hqT_all = sb("hqT_t", [128, 22, NTI * 128], BF16)
        sgb = sb("sgb_t", [128, 512], BF16)
        lnm = sb("lnm_t", [128, NTI, 1024], BF16)
        lnmT = sb("lnmT_t", [128, 8, TC], BF16)
        xcv = sb("xcv_t", [128, 16, TC + 3], BF16)
        zsl = sb("zsl_t", [128, 16, TC], BF16)
        u16 = sb("u16_t", [128, 16, TC], BF16)
        hbuf = sb("hbuf_t", [128, 16, TC], BF16)
        yg = sb("yg_t", [128, 16, TC], BF16)
        Bbc = sb("Bbc_t", [128, 16, TC], BF16)
        Cbc = sb("Cbc_t", [128, 16, TC], BF16)
        dtr = sb("dtr_t", [128, TC], BF16)
        bcsb = sb("bcsb_t", [48, 2 * TC], BF16)
        delta2 = sb("delta2_t", [128, 2, TC], F32)
        du2 = sb("du2_t", [128, 2, TC], BF16)
        accA = sb("accA_t", [128, TC], F32)
        accB = sb("accB_t", [128, TC], F32)
        yv = sb("yv_t", [128, TC], F32)
        y2v = sb("y2v_t", [128, TC], F32)
        wreg = sb("wreg_t", [128, 16384], BF16)

        gmemset(xcv[:, :, :].rearrange("p a b -> p (a b)"), 0.0)
        pr.join()

        wranges = []

        def wfence(off, size):
            w = [t for (o, sz, t) in wranges if o < off + size and off < o + sz]
            return w

        def wmark(off, size, tok):
            wranges[:] = [(o, sz, t) for (o, sz, t) in wranges
                          if not (o >= off and o + sz <= off + size)]
            wranges.append((off, size, tok))

        def wview(off, n_a, n_b):
            return wreg[:, off:off + n_a * n_b].rearrange("p (a b) -> p a b", a=n_a)

        def ln_stats_group(src, w_in):
            w = list(w_in)
            for ti in range(NTI):
                E("vector", lambda h, i=src[:, ti, 0:512], o=bnb[:, ti * 12:ti * 12 + 6]:
                  h.bn_stats(o, i), waits=w + [("s", pr.cnt["s"])])
                w = []
                tl = E("vector", lambda h, i=src[:, ti, 512:1024],
                       o=bnb[:, ti * 12 + 6:ti * 12 + 12]: h.bn_stats(o, i))
            pr.emit("vector", lambda h: h.nop(), waits=[tl])
            for ti in range(NTI):
                ta = E("vector", lambda h, o=stat[:, ti, 0:2], i=bnb[:, ti * 12:ti * 12 + 12]:
                       h.bn_aggr(o, i))
            pr.emit("vector", lambda h: h.nop(), waits=[ta])
            vvar = stat[:, :, 1]
            vr = stat[:, :, 2]
            tv = ts(vr, vvar, 1e-5, None, OP.add, waits=[("s", pr.cnt["s"])])
            tl = act(vr, vr, AF.Ln, waits=[tv])
            return act(vr, vr, AF.Exp, scale=-0.5, waits=[tl])

        def ln_apply(ti, xin, out_v, w=()):
            return ts(out_v, xin, stat[:, ti, 0:1], stat[:, ti, 2:3],
                      OP.subtract, OP.mult, waits=w)

        I_GROUPS = [(0, 512), (512, 512), (1024, 512), (1536, 512),
                    (2048, 512), (2560, 256)]
        H_SL = [(i * 256, 256) for i in range(4)]

        def ffn_stage(tx):
            tst = ln_stats_group(xres, [tx])
            for ti in range(NTI):
                ta = ln_apply(ti, xres[:, ti, :], A1[:, 0:1024], [tst])
                quant(A1[:, 0:1024], xq[:, ti, :], 1024, [ta], cbuf=h_f[:, ti, :])
            for ti in range(NTI):
                pb = pt_b[ti % 2]
                wts = fence_ps(("pt", ti % 2)) + [("v", pr.cnt["v"])]
                for dc in range(8):
                    trans(pb[:, dc, :], xq[:, ti, dc * 128:(dc + 1) * 128], waits=wts)
                    wts = []
                ev = scopy(xqT[:, :, ti * 128:(ti + 1) * 128], pb[:, :, :],
                           waits=[("t", pr.cnt["t"])])
                prd[("pt", ti % 2)] = ev
            txqT = ("s", pr.cnt["s"])
            gtok = {}
            for gi in range(2):
                i0, iw = I_GROUPS[gi]
                go, uo = (gi % 2) * 4096, 8192 + (gi % 2) * 4096
                gtok[("g", gi)] = (go, dma(wview(go, 8, iw), wgq_s[:, :, i0:i0 + iw],
                                           waits=wfence(go, 8 * iw),
                                           qsem="qg%d" % (gi % 2)))
                gtok[("u", gi)] = (uo, dma(wview(uo, 8, iw), wuq_s[:, :, i0:i0 + iw],
                                           waits=wfence(uo, 8 * iw),
                                           qsem="qu%d" % (gi % 2)))
            thp = {"tok": None}

            def gateup_group(gi):
                i0, iw = I_GROUPS[gi]
                go, twg = gtok[("g", gi)]
                uo, twu = gtok[("u", gi)]
                for ti in range(NTI):
                    pg, pu = pmm[0], pmm[1]
                    wts = fence_ps(("pmm", 0)) + [twg, txqT]
                    for dc in range(8):
                        mm(pg[:, 0:iw], xqT[:, dc, ti * 128:(ti + 1) * 128],
                           wview(go, 8, iw)[:, dc, :], dc == 0, dc == 7, waits=wts)
                        wts = []
                    tg = ("t", pr.cnt["t"])
                    wts = fence_ps(("pmm", 1)) + [twu]
                    for dc in range(8):
                        mm(pu[:, 0:iw], xqT[:, dc, ti * 128:(ti + 1) * 128],
                           wview(uo, 8, iw)[:, dc, :], dc == 0, dc == 7, waits=wts)
                        wts = []
                    tu = ("t", pr.cnt["t"])
                    tsg = act(sgb[:, 0:iw], pg[:, 0:iw], AF.Silu,
                              waits=[tg, thp["tok"]])
                    prd[("pmm", 0)] = tsg
                    th = tt(h_f[:, ti, i0:i0 + iw], sgb[:, 0:iw],
                            pu[:, 0:iw], OP.mult, waits=[tu, tsg])
                    prd[("pmm", 1)] = th
                    thp["tok"] = th
                tmk = ("t", pr.cnt["t"])
                wmark(go, 8 * iw, tmk)
                wmark(uo, 8 * iw, tmk)
                if gi + 2 < len(I_GROUPS):
                    i2, iw2 = I_GROUPS[gi + 2]
                    go2, uo2 = (gi % 2) * 4096, 8192 + (gi % 2) * 4096
                    gtok[("g", gi + 2)] = (go2, dma(wview(go2, 8, iw2),
                                                    wgq_s[:, :, i2:i2 + iw2],
                                                    waits=wfence(go2, 8 * iw2),
                                                    qsem="qg%d" % (gi % 2)))
                    gtok[("u", gi + 2)] = (uo2, dma(wview(uo2, 8, iw2),
                                                    wuq_s[:, :, i2:i2 + iw2],
                                                    waits=wfence(uo2, 8 * iw2),
                                                    qsem="qu%d" % (gi % 2)))
            for gi in (0, 1, 2):
                gateup_group(gi)
            tq_half = {}
            for ti in range(NTI):
                tq_half[(ti, 0)] = quant(h_f[:, ti, 0:1408], hq_ti2[:, ti, 0:1408],
                                         1408, [], cbuf=A1)
            for gi in (3, 4, 5):
                gateup_group(gi)
            for ti in range(NTI):
                tq_half[(ti, 1)] = quant(h_f[:, ti, 1408:2816],
                                         hq_ti2[:, ti, 1408:2816], 1408, [], cbuf=A1)
            for ti in range(NTI):
                tqh = tq_half[(ti, 1)]
                for g3 in range(3):
                    ics = list(range(8 * g3, min(22, 8 * g3 + 8)))
                    pb = pt_b[g3 % 2]
                    wts = fence_ps(("pt", g3 % 2)) + [tq_half[(ti, 0)], tqh]
                    for j, ic in enumerate(ics):
                        trans(pb[:, j, :], hq_ti2[:, ti, ic * 128:(ic + 1) * 128],
                              waits=wts)
                        wts = []
                    ev = scopy(hqT_all[:, ics[0]:ics[-1] + 1, ti * 128:(ti + 1) * 128],
                               pb[:, 0:len(ics), :], waits=[("t", pr.cnt["t"])])
                    prd[("pt", g3 % 2)] = ev
            tT = ("s", pr.cnt["s"])
            wd_tok = {}
            for si in range(2):
                o0, ow = H_SL[si]
                off = (si % 2) * 5632
                wd_tok[si] = dma(wview(off, 22, ow), wdq_s[:, :, o0:o0 + ow],
                                 waits=wfence(off, 22 * ow), qsem="qd%d" % (si % 2))
            for si, (o0, ow) in enumerate(H_SL):
                off = (si % 2) * 5632
                twd = wd_tok[si]
                for ti in range(NTI):
                    pd = pmm[2 + si % 2]
                    wts = fence_ps(("pmm", 2 + si % 2)) + [twd, tT]
                    for ic in range(22):
                        mm(pd[:, 0:ow], hqT_all[:, ic, ti * 128:(ti + 1) * 128],
                           wview(off, 22, ow)[:, ic, :], ic == 0, ic == 21,
                           waits=wts)
                        wts = []
                    tre = stt(xres[:, ti, o0:o0 + ow], pd[:, 0:ow], 0.5,
                              xres[:, ti, o0:o0 + ow], OP.mult, OP.add,
                              waits=[("t", pr.cnt["t"])])
                    prd[("pmm", 2 + si % 2)] = tre
                wmark(off, 22 * ow, ("t", pr.cnt["t"]))
                if si + 2 < len(H_SL):
                    o2, ow2 = H_SL[si + 2]
                    wd_tok[si + 2] = dma(wview(off, 22, ow2),
                                         wdq_s[:, :, o2:o2 + ow2],
                                         waits=wfence(off, 22 * ow2),
                                         qsem="qd%d" % (si % 2))

        def mamba_stage(tx, skip_out=False):
            scr = h_f
            tst = ln_stats_group(xres, [tx])
            for ti in range(NTI):
                ln_apply(ti, xres[:, ti, :], scr[:, ti, 0:1024], [tst])
            tst2 = ln_stats_group(scr[:, :, 0:1024], [("v", pr.cnt["v"])])
            for ti in range(NTI):
                tb = ln_apply(ti, scr[:, ti, 0:1024], A2[:, 0:1024], [tst2])
                tc = tt(A3[:, 0:1024], A2[:, 0:1024], gam[:, :], OP.mult, waits=[tb])
                tt(lnm[:, ti, :], A3[:, 0:1024], bet[:, :], OP.add, waits=[tc])
            for ti in range(NTI):
                pb = pt_b[ti % 2]
                wts = fence_ps(("pt", ti % 2)) + [("v", pr.cnt["v"])]
                for kc in range(8):
                    trans(pb[:, kc, :], lnm[:, ti, kc * 128:(kc + 1) * 128], waits=wts)
                    wts = []
                ev = scopy(lnmT[:, :, ti * 128:(ti + 1) * 128], pb[:, :, :],
                           waits=[("t", pr.cnt["t"])])
                prd[("pt", ti % 2)] = ev
            if MAMBA_CUT == 1:
                return
            tT = ("s", pr.cnt["s"])
            wip_tok = {}
            for sl in range(2):
                off = (sl % 2) * 8192
                wip_tok[sl] = dma(wview(off, 8, 1024),
                                  wip_s[:, :, sl * 1024:(sl + 1) * 1024],
                                  waits=wfence(off, 8192), qsem="qi%d" % (sl % 2))
            for sl in range(4):
                off = (sl % 2) * 8192
                twip = wip_tok[sl]
                for mc8 in range(8):
                    m = sl * 8 + mc8
                    pd = pmm[m % 2]
                    wts = fence_ps(("pmm", m % 2)) + [twip, tT]
                    for kc in range(8):
                        mm(pd[:, 0:TC], wview(off, 8, 1024)[:, kc, mc8 * 128:(mc8 + 1) * 128],
                           lnmT[:, kc, :], kc == 0, kc == 7, waits=wts)
                        wts = []
                    tmv = ("t", pr.cnt["t"])
                    if m < 16:
                        ev = scopy(xcv[:, m, 3:TC + 3], pd[:, 0:TC], waits=[tmv])
                    else:
                        ev = act(zsl[:, m - 16, :], pd[:, 0:TC], AF.Silu, waits=[tmv])
                    prd[("pmm", m % 2)] = ev
                wmark(off, 8192, ("t", pr.cnt["t"]))
                if sl + 2 < 4:
                    wip_tok[sl + 2] = dma(wview(off, 8, 1024),
                                          wip_s[:, :, (sl + 2) * 1024:(sl + 3) * 1024],
                                          waits=wfence(off, 8192),
                                          qsem="qi%d" % (sl % 2))
            tz = ("s", pr.cnt["s"])
            for dc in range(16):
                w_in = [tz] if dc == 0 else []
                ts(accA[:, :], xcv[:, dc, 0:TC], convw[:, dc, 0:1], None,
                   OP.mult, OP.bypass, waits=w_in)
                stt(accB[:, :], xcv[:, dc, 1:TC + 1], convw[:, dc, 1:2], accA[:, :])
                stt(accA[:, :], xcv[:, dc, 2:TC + 2], convw[:, dc, 2:3], accB[:, :])
                t3 = stt(accB[:, :], xcv[:, dc, 3:TC + 3], convw[:, dc, 3:4], accA[:, :])
                act(u16[:, dc, :], accB[:, :], AF.Silu, bias=convb[:, dc:dc + 1],
                    waits=[t3])
                vcopy(xcv[:, dc, 0:3], xcv[:, dc, TC:TC + 3])
            if MAMBA_CUT == 2:
                return
            tu_all = ("s", pr.cnt["s"])
            pxp = pmm[2]
            wts = fence_ps(("pmm", 2)) + [tu_all]
            for kc in range(16):
                mm(pxp[:, 0:TC], xpw[:, kc, :], u16[:, kc, :], kc == 0, kc == 15,
                   waits=wts)
                wts = []
            txp = ("t", pr.cnt["t"])
            e1 = scopy(dtr[64:128, :], pxp[64:128, 0:TC], waits=[txp])
            scopy(bcsb[0:16, 0:TC], pxp[0:16, 0:TC])
            e3 = scopy(bcsb[32:48, 0:TC], pxp[32:48, 0:TC])
            prd[("pmm", 2)] = e3
            for n in range(16):
                for which in range(2):
                    bk = 2 + ((n * 2 + which) % 2)
                    pb2 = pmm[bk]
                    wts = fence_ps(("pmm", bk)) + [e3]
                    p0 = 0 if which == 0 else 32
                    mm(pb2[:, 0:TC], sel[p0:p0 + 16, n * 128:(n + 1) * 128],
                       bcsb[p0:p0 + 16, 0:TC], True, True, waits=wts)
                    dst = Bbc if which == 0 else Cbc
                    ev = scopy(dst[:, n, :], pb2[:, 0:TC], waits=[("t", pr.cnt["t"])])
                    prd[("pmm", bk)] = ev
            if MAMBA_CUT == 3:
                return
            tbc = ("s", pr.cnt["s"])
            par_tok = {0: [], 1: []}
            scan_tok = {k: None for k in range(8)}
            for dc in range(16):
                db = dc % 2
                pd = pmm[dc % 4]
                wts = fence_ps(("pmm", dc % 4)) + ([e1] if dc == 0 else []) + par_tok[db]
                mm(pd[:, 0:TC], dtw[64:128, dc * 128:(dc + 1) * 128], dtr[64:128, :],
                   True, True, waits=wts)
                tdt = ("t", pr.cnt["t"])
                dl = delta2[:, db, :]
                duv = du2[:, db, :]
                tm0 = ts(dl, pd[:, 0:TC], dtb[:, dc:dc + 1], 40.0, OP.add, OP.min,
                         waits=[tdt] + par_tok[db])
                tm1 = act(dl, dl, AF.Exp, waits=[tm0])
                tm2 = act(dl, dl, AF.Ln, bias=1.0, waits=[tm1])
                tm3 = stt(dl, pd[:, 0:TC], dtb[:, dc:dc + 1], dl, OP.add, OP.max,
                          waits=[tm2])
                prd[("pmm", dc % 4)] = tm3
                tdu = tt(duv, dl, u16[:, dc, :], OP.mult, waits=[tm3])
                dball = h_f[:, 0, 0:2048].bitcast(BF16).rearrange(
                    "p (a b) -> p a b", a=16)
                dA8 = h_f[:, 1, 0:2048].rearrange("p (a b) -> p a b", a=8)
                duv_b = AP(duv.tensor, duv.offset,
                           [list(duv.ap[0]), [0, 16], list(duv.ap[1])])
                tt(dball, duv_b, Bbc[:, :, :], OP.mult, waits=[tdu, tbc])
                for n in range(16):
                    nb = n % 8
                    tda = act(dA8[:, nb, :], dl, AF.Exp,
                              scale=A_t[:, dc * 16 + n:dc * 16 + n + 1],
                              waits=[scan_tok[nb], tm3])
                    tsc = scan_op(hbuf[:, n, :], dA8[:, nb, :], dball[:, n, :],
                                  hcar[:, dc * 16 + n:dc * 16 + n + 1],
                                  waits=[tda])
                    scan_tok[nb] = tsc
                tcar = vcopy(hcar[:, dc * 16:dc * 16 + 16],
                             hbuf[:, :, TC - 1:TC].rearrange("p a b -> p (a b)"))
                tpr = tt(hbuf[:, :, :].rearrange("p a b -> p (a b)"),
                         hbuf[:, :, :].rearrange("p a b -> p (a b)"),
                         Cbc[:, :, :].rearrange("p a b -> p (a b)"), OP.mult,
                         waits=[tcar])
                tyr = red(yv[:, :], hbuf[:, :, :].transpose([0, 2, 1]), OP.add,
                          waits=[tpr])
                for _k in range(8):
                    scan_tok[_k] = tyr
                ty2 = stt(y2v[:, :], u16[:, dc, :], D_t[:, dc:dc + 1], yv[:, :],
                          OP.mult, OP.add, waits=[tyr])
                tt(yg[:, dc, :], y2v[:, :], zsl[:, dc, :], OP.mult, waits=[ty2])
                par_tok[db] = [("s", pr.cnt["s"]), ("v", pr.cnt["v"]),
                               ("g", pr.cnt["g"])]
            if MAMBA_CUT == 4 or skip_out:
                return
            tyg_all = ("v", pr.cnt["v"])
            wo_tok = {}
            for nc_i in range(2):
                off = nc_i * 8192
                wo_tok[nc_i] = dma(wview(off, 16, 512),
                                   wo_s[:, :, nc_i * 512:(nc_i + 1) * 512],
                                   waits=wfence(off, 8192), qsem="qw%d" % (nc_i % 2))
            for nc_i in range(2):
                off = nc_i * 8192
                two = wo_tok[nc_i]
                for ti in range(NTI):
                    pd = pmm[ti % 2]
                    wts = fence_ps(("pmm", ti % 2)) + [two, tyg_all]
                    for kc in range(16):
                        mm(pd[:, :], yg[:, kc, ti * 128:(ti + 1) * 128],
                           wview(off, 16, 512)[:, kc, :], kc == 0, kc == 15, waits=wts)
                        wts = []
                    tre = tt(xres[:, ti, nc_i * 512:(nc_i + 1) * 512],
                             xres[:, ti, nc_i * 512:(nc_i + 1) * 512], pd[:, :],
                             OP.add, waits=[("t", pr.cnt["t"])])
                    prd[("pmm", ti % 2)] = tre
                wmark(off, 8192, ("t", pr.cnt["t"]))

        dbg = {}
        if DEV_DEBUG:
            dbg["xq"] = nc.dram_tensor("dbg_xq", [128, NTI, 1024], BF16,
                                       kind="ExternalOutput").ap()
            dbg["hf"] = nc.dram_tensor("dbg_hf", [128, NTI, 2816], F32,
                                       kind="ExternalOutput").ap()
            dbg["wg"] = nc.dram_tensor("dbg_wg", [128, 8, 2816], BF16,
                                       kind="ExternalOutput").ap()
            dbg["xqT"] = nc.dram_tensor("dbg_xqT", [128, 8, NTI * 128], BF16,
                                        kind="ExternalOutput").ap()
            dbg["ln"] = nc.dram_tensor("dbg_ln", [128, 1024], F32,
                                       kind="ExternalOutput").ap()
            dbg["stat"] = nc.dram_tensor("dbg_stat", [128, NTI, 4], F32,
                                         kind="ExternalOutput").ap()
            dbg["sv"] = nc.dram_tensor("dbg_sv", [128, 88], F32,
                                       kind="ExternalOutput").ap()
            dbg["Bbc"] = nc.dram_tensor("dbg_Bbc", [128, 16, TC], BF16,
                                        kind="ExternalOutput").ap()
            dbg["Cbc"] = nc.dram_tensor("dbg_Cbc", [128, 16, TC], BF16,
                                        kind="ExternalOutput").ap()
            dbg["u16"] = nc.dram_tensor("dbg_u16", [128, 16, TC], BF16,
                                        kind="ExternalOutput").ap()
            dbg["zsl"] = nc.dram_tensor("dbg_zsl", [128, 16, TC], BF16,
                                        kind="ExternalOutput").ap()
            dbg["dtr"] = nc.dram_tensor("dbg_dtr", [128, TC], BF16,
                                        kind="ExternalOutput").ap()
            dbg["yg"] = nc.dram_tensor("dbg_yg", [128, 16, TC], BF16,
                                       kind="ExternalOutput").ap()
            dbg["hb"] = nc.dram_tensor("dbg_hb", [128, 16, TC], BF16,
                                       kind="ExternalOutput").ap()
        for c in range(DEV_NCH or NCH):
            tx = dma(xres[:, :, :],
                     x_d[c * TC:(c + 1) * TC, :].rearrange("(a p) d -> p a d", p=128),
                     qsem="qx", waits=[("qy", pr.cnt["qy"]), ("v", pr.cnt["v"])])
            if DEV_STAGES >= 1 and DEV_STAGES != 5:
                ffn_stage(tx)
            else:
                E("vector", lambda h: h.nop(), waits=[tx])
            if DEV_DEBUG and c == 0:
                w_all = [(s_, pr.cnt[s_]) for s_ in ("v", "s", "t")]
                dma(dbg["xq"], xq[:, :, :], waits=w_all, qsem="qo")
                dma(dbg["ln"], A1[:, 0:1024], waits=w_all, qsem="qo")
                dma(dbg["stat"], stat[:, :, :], waits=w_all, qsem="qo")
                dma(dbg["sv"], sv[:, :], waits=w_all, qsem="qo")
                dma(dbg["hf"], h_f[:, :, :], waits=w_all, qsem="qo")
                dma(dbg["xqT"], xqT[:, :, :], waits=w_all, qsem="qo")
                tb = dma(wview(0, 8, 1024), wgq_s[:, :, 0:1024], waits=fence_ps(("wr", 0)))
                dma(dbg["wg"][:, :, 0:1024], wview(0, 8, 1024), waits=[tb], qsem="qo")
                prd[("wr", 0)] = ("t", pr.cnt["t"])
            if DEV_STAGES >= 2 or DEV_STAGES == 5:
                mamba_stage(tx, skip_out=(c in MAMBA_SKIP_OUT))
                if DEV_DEBUG and c == 0:
                    w_all = [(s_, pr.cnt[s_]) for s_ in ("v", "s", "t", "g")]
                    for k_, b_ in (("Bbc", Bbc), ("Cbc", Cbc), ("u16", u16),
                                   ("zsl", zsl), ("yg", yg), ("hb", hbuf)):
                        if k_ in dbg:
                            dma(dbg[k_], b_[:, :, :], waits=w_all, qsem="qo")
                    dma(dbg["dtr"], dtr[:, :], waits=w_all, qsem="qo")
            if DEV_STAGES >= 3:
                ffn_stage(tx)
            dma(y_d[c * TC:(c + 1) * TC, :].rearrange("(a p) d -> p a d", p=128),
                xres[:, :, :],
                waits=[("v", pr.cnt["v"]), ("s", pr.cnt["s"]), ("t", pr.cnt["t"])],
                qsem="qy")
        pr.emit("sync", lambda h: h.nop(), waits=[("qy", pr.cnt["qy"])])
        with nc.Block() as block:
            def runner(eng):
                def go(h):
                    for ws, fn, inc, k in pr.q[eng]:
                        for s_, v_ in ws:
                            h.wait_ge(sem[s_], v_)
                        op = fn(h)
                        if inc is not None:
                            op.then_inc(sem[inc], k)
                return go
            block.sync(runner("sync"))
            block.scalar(runner("scalar"))
            block.vector(runner("vector"))
            block.tensor(runner("tensor"))
            block.gpsimd(runner("gpsimd"))
    return nc


def _host_layout(inp):
    f = np.float32
    g = lambda k: np.asarray(inp[k], dtype=f)
    d = {}
    d["wip"] = np.ascontiguousarray(g("L0_attn_in_proj_w").reshape(8, 128, 4096).transpose(1, 0, 2))
    d["wo"] = np.ascontiguousarray(g("L0_attn_out_proj_w").reshape(16, 128, 1024).transpose(1, 0, 2))
    xpw_r = g("L0_attn_x_proj_w")
    z16 = np.zeros((2048, 16), np.float32)
    xpw_r = np.concatenate([xpw_r[:, 64:80], z16, xpw_r[:, 80:96], z16,
                            xpw_r[:, 0:64]], axis=1)
    d["xpw"] = np.ascontiguousarray(xpw_r.reshape(16, 128, 128).transpose(1, 0, 2))
    d["dtw"] = np.ascontiguousarray(g("L0_attn_dt_proj_w"))
    d["wgt"] = np.ascontiguousarray(g("L0_ffn_w_gate").reshape(22, 128, 1024).transpose(1, 0, 2))
    d["wut"] = np.ascontiguousarray(g("L0_ffn_w_up").reshape(22, 128, 1024).transpose(1, 0, 2))
    d["wdt"] = np.ascontiguousarray(g("L0_ffn_w_down").reshape(8, 128, 2816).transpose(1, 0, 2))
    d["convw"] = np.ascontiguousarray(g("L0_attn_conv_w").reshape(16, 128, 4).transpose(1, 0, 2))
    d["convb"] = np.ascontiguousarray(g("L0_attn_conv_b").reshape(16, 128).T)
    d["alog"] = np.ascontiguousarray(g("L0_attn_A_log").reshape(16, 128, 16).transpose(1, 0, 2))
    d["dp"] = np.ascontiguousarray(g("L0_attn_D_param").reshape(16, 128).T)
    d["dtb"] = np.ascontiguousarray(g("L0_attn_dt_proj_b").reshape(16, 128).T)
    d["gam"] = np.ascontiguousarray(np.broadcast_to(g("L0_attn_norm_w"), (128, 1024)))
    d["bet"] = np.ascontiguousarray(np.broadcast_to(g("L0_attn_norm_b"), (128, 1024)))
    return d


def kernel(**inputs):
    hs = np.ascontiguousarray(np.asarray(inputs["L0_attn_hidden_states"], dtype=np.float32))
    shared = _host_layout(inputs)
    nc = build_nc()
    in_maps = [dict(shared, x=np.ascontiguousarray(hs[c])) for c in range(8)]
    res = run_bass_kernel_spmd(nc, in_maps, core_ids=list(range(8)))
    out = np.stack([np.asarray(res.results[c]["y"]) for c in range(8)], axis=0)
    return out.astype(np.float32)


# version 18
# speedup vs baseline: 1.0078x; 1.0003x over previous
import numpy as np
from contextlib import ExitStack

import concourse.bass as bass
import concourse.mybir as mybir
from concourse.bass import AP
from concourse.bass_utils import run_bass_kernel_spmd

F32 = mybir.dt.float32
BF16 = mybir.dt.bfloat16
AF = mybir.ActivationFunctionType
OP = mybir.AluOpType
AX = mybir.AxisListType

TC = 256
NCH = 4096 // TC
NTI = TC // 128
DEV_STAGES = 3
DEV_DEBUG = False
MAMBA_CUT = 0
DEV_NCH = None
MAMBA_SKIP_OUT = ()
DBU_ENG = "vector"

from concourse.dve_ops import DveOp, OPS as _DVE_OPS
from concourse.dve_spec import Spec, Src0, C0, C1, C2, sq as _sq, lower as _dve_lower


def _register_dve(name, spec, ref):
    from concourse import dve_ops as _dvo
    have = {o.name for o in _DVE_OPS}
    if name in have:
        return next(o for o in _DVE_OPS if o.name == name)
    shas = {}
    for ver in ("v3", "v4"):
        try:
            shas[ver] = _dve_lower(spec, ver=ver).sha(ver)
        except Exception:
            pass
    op = DveOp(name, spec, False, shas, reference=ref)
    _DVE_OPS.append(op)
    _dvo._SUB_OPCODE_FOR_NAME[name] = _dvo._CUSTOM_DVE_ROW_BASE + len(_DVE_OPS) - 1
    _dvo.CUSTOM_DVE_SPECS[name] = spec
    _dvo.CUSTOM_DVE_REFS[name] = ref
    return op


def _qc_ref(in0, in1, s0, s1, imm2):
    sqv = in0 * in0
    c = (sqv >= s0).astype(np.float32) + 2.0 * (sqv >= s1).astype(np.float32)
    return in0 + (c * imm2 + imm2)


def _qd_ref(in0, in1, s0, s1, imm2):
    c = (in0 >= s0).astype(np.float32) + 2.0 * (in0 >= s1).astype(np.float32)
    return in0 - (c * imm2 + imm2)


_s_ = _sq(Src0)
ANT_QC = _register_dve(
    "ANT_QC", Spec(body=Src0 + (((_s_ >= C0) + ((_s_ >= C1) + (_s_ >= C1))) * C2 + C2)),
    _qc_ref)
ANT_QD = _register_dve(
    "ANT_QD", Spec(body=Src0 - (((Src0 >= C0) + ((Src0 >= C1) + (Src0 >= C1))) * C2 + C2)),
    _qd_ref)

ENGS = ("sync", "scalar", "vector", "tensor", "gpsimd")
ESEM = {"scalar": "s", "vector": "v", "tensor": "t", "gpsimd": "g", "sync": "y0"}
SEMS = ["s", "v", "t", "g", "y0", "qw", "qx", "qo", "qp",
        "qg0", "qg1", "qu0", "qu1", "qd0", "qd1", "qi0", "qi1",
        "qw0", "qw1", "qpa", "qpb", "qy"]


class Pr:

    def __init__(self):
        self.q = {e: [] for e in ENGS}
        self.cnt = {s: 0 for s in SEMS}
        self.lastw = {}

    def emit(self, eng, fn, waits=(), inc=None, k=1):
        ws = []
        for w in waits:
            if w is None:
                continue
            s, v = w
            if v is None or v <= 0:
                continue
            key = (eng, s)
            if self.lastw.get(key, 0) >= v:
                continue
            self.lastw[key] = v
            ws.append((s, v))
        if inc is not None:
            self.cnt[inc] += k
        self.q[eng].append((tuple(ws), fn, inc, k))
        return (inc, self.cnt[inc]) if inc else None

    DMAS = ("qw", "qx", "qo", "qp", "qg0", "qg1", "qu0", "qu1", "qd0", "qd1",
            "qi0", "qi1", "qw0", "qw1", "qpa", "qpb", "qy")

    def join(self, extra_sems=()):
        for e in ENGS:
            self.emit(e, lambda h: h.nop(), inc=ESEM[e])
        c = dict(self.cnt)
        allw = [(ESEM[e], c[ESEM[e]]) for e in ENGS]
        allw += [(s, c[s]) for s in self.DMAS]
        for e in ENGS:
            self.emit(e, lambda h: h.nop(), waits=allw)

    def prune_incs(self):
        import os
        comp = set(os.environ.get("PRUNE_SEMS", "s,v,t,g,y0").split(","))
        waited = {s: set() for s in self.cnt}
        for e in ENGS:
            for ws, _, _, _ in self.q[e]:
                for s, v in ws:
                    waited[s].add(v)
        rank = {s: {v: i + 1 for i, v in enumerate(sorted(waited[s]))} for s in comp}
        stats = [0, 0]
        for e in ENGS:
            s_e = ESEM[e]
            c = 0
            nq = []
            for ws, fn, inc, k in self.q[e]:
                nws = tuple((s, rank[s][v]) if s in comp else (s, v) for s, v in ws)
                if inc == s_e and s_e in comp:
                    c += k
                    stats[1] += 1
                    if c in waited[s_e]:
                        stats[0] += 1
                        nq.append((nws, fn, inc, 1))
                    else:
                        nq.append((nws, fn, None, 0))
                else:
                    nq.append((nws, fn, inc, k))
            self.q[e] = nq
        return stats


def build_nc():
    nc = bass.Bass("TRN2", target_bir_lowering=False)
    pr = Pr()

    x_d = nc.dram_tensor("x", [4096, 1024], F32, kind="ExternalInput").ap()
    y_d = nc.dram_tensor("y", [4096, 1024], F32, kind="ExternalOutput").ap()
    wip_d = nc.dram_tensor("wip", [128, 8, 4096], F32, kind="ExternalInput").ap()
    wo_d = nc.dram_tensor("wo", [128, 16, 1024], F32, kind="ExternalInput").ap()
    xpw_d = nc.dram_tensor("xpw", [128, 16, 128], F32, kind="ExternalInput").ap()
    dtw_d = nc.dram_tensor("dtw", [64, 2048], F32, kind="ExternalInput").ap()
    wgt_d = nc.dram_tensor("wgt", [128, 22, 1024], F32, kind="ExternalInput").ap()
    wut_d = nc.dram_tensor("wut", [128, 22, 1024], F32, kind="ExternalInput").ap()
    wdt_d = nc.dram_tensor("wdt", [128, 8, 2816], F32, kind="ExternalInput").ap()
    convw_d = nc.dram_tensor("convw", [128, 16, 4], F32, kind="ExternalInput").ap()
    convb_d = nc.dram_tensor("convb", [128, 16], F32, kind="ExternalInput").ap()
    alog_d = nc.dram_tensor("alog", [128, 16, 16], F32, kind="ExternalInput").ap()
    dp_d = nc.dram_tensor("dp", [128, 16], F32, kind="ExternalInput").ap()
    dtb_d = nc.dram_tensor("dtb", [128, 16], F32, kind="ExternalInput").ap()
    gam_d = nc.dram_tensor("gam", [128, 1024], F32, kind="ExternalInput").ap()
    bet_d = nc.dram_tensor("bet", [128, 1024], F32, kind="ExternalInput").ap()
    wip_s = nc.dram_tensor("wip_s", [128, 8, 4096], BF16, kind="Internal").ap()
    wo_s = nc.dram_tensor("wo_s", [128, 16, 1024], BF16, kind="Internal").ap()
    wgq_s = nc.dram_tensor("wgq_s", [128, 8, 2816], BF16, kind="Internal").ap()
    wuq_s = nc.dram_tensor("wuq_s", [128, 8, 2816], BF16, kind="Internal").ap()
    wdq_s = nc.dram_tensor("wdq_s", [128, 22, 1024], BF16, kind="Internal").ap()

    st = ExitStack()
    sb = lambda n, sh, dt: st.enter_context(nc.sbuf_tensor(n, sh, dt))
    ps = lambda n, sh, dt: st.enter_context(nc.psum_tensor(n, sh, dt))

    with st:
        sem = {n: st.enter_context(nc.semaphore(n)) for n in SEMS}

        convw = sb("convw_t", [128, 16, 4], F32)
        convb = sb("convb_t", [128, 16], F32)
        A_t = sb("A_t", [128, 256], F32)
        D_t = sb("D_t", [128, 16], F32)
        dtb = sb("dtb_t", [128, 16], F32)
        gam = sb("gam_t", [128, 1024], BF16)
        bet = sb("bet_t", [128, 1024], BF16)
        ident = sb("ident_t", [128, 128], BF16)
        ones1 = sb("ones1_t", [1, 128], BF16)
        hcar = sb("hcar_t", [128, 256], F32)
        xpw = sb("xpw_t", [128, 16, 128], BF16)
        dtw = sb("dtw_t", [128, 2048], BF16)
        stat = sb("stat_t", [128, NTI, 4], F32)
        bnb = sb("bnb_t", [128, 24], F32)
        bmv = sb("bmv_t", [128, 88], F32)
        sv = sb("sv_t", [128, 88], F32)
        iv = sb("iv_t", [128, 88], F32)
        A1 = sb("A1_t", [128, 2048], F32)
        A2 = sb("A2_t", [128, 1408], F32)
        A3 = sb("A3_t", [128, 1408], F32)
        sel = sb("sel_t", [48, 2048], BF16)
        maskc = sb("maskc_t", [128, 1], mybir.dt.int32)

        pmm = [ps("pmm%d" % i, [128, 512], F32) for i in range(4)]
        pt_b = [ps("ptb%d" % i, [128, 8, 128], BF16) for i in range(2)]
        prd = {}

        def fence_ps(key):
            t = prd.get(key)
            return [t] if t else []

        def E(eng, fn, waits=(), k=1):
            return pr.emit(eng, fn, waits, ESEM[eng], k)

        def dma(out, in_, waits=(), qsem="qw", eng="sync"):
            pr.emit(eng, lambda h, o=out, i=in_: h.dma_start(o, i), waits, qsem, 16)
            return (qsem, pr.cnt[qsem])

        def tt(out, a, b, op=OP.mult, waits=(), eng="vector"):
            return E(eng, lambda h, o=out, x=a, yy=b, p=op: h.tensor_tensor(o, x, yy, p), waits)

        def ts(out, in0, s1, s2=None, op0=OP.mult, op1=OP.bypass, waits=(), eng="vector"):
            return E(eng, lambda h, o=out, i=in0, a=s1, b=s2, p0=op0, p1=op1:
                     h.tensor_scalar(o, i, a, b, p0, p1), waits)

        def stt(out, in0, sc, in1, op0=OP.mult, op1=OP.add, waits=()):
            return E("vector", lambda h, o=out, i=in0, s_=sc, j=in1, p0=op0, p1=op1:
                     h.scalar_tensor_tensor(o, i, s_, j, p0, p1), waits)

        def act(out, in_, func, bias=0.0, scale=1.0, waits=()):
            return E("scalar", lambda h, o=out, i=in_, f=func, b=bias, sc=scale:
                     h.activation(o, i, f, bias=b, scale=sc), waits)

        def scopy(out, in_, waits=()):
            return E("scalar", lambda h, o=out, i=in_: h.copy(o, i), waits)

        def vcopy(out, in_, waits=()):
            return E("vector", lambda h, o=out, i=in_: h.tensor_copy(o, i), waits)

        def mm(out, lhsT, rhs, start, stop, waits=()):
            return E("tensor", lambda h, o=out, l=lhsT, r=rhs, s0=start, s1=stop:
                     h.matmul(o, l, r, start=s0, stop=s1), waits)

        def trans(out, in_, waits=()):
            return E("tensor", lambda h, o=out, i=in_: h.transpose(o, i, ident[:, :]), waits)

        def scan_op(out, d0, d1, init, waits=()):
            return E("vector", lambda h, o=out, a=d0, b=d1, ii=init:
                     h.tensor_tensor_scan(o, a, b, ii, OP.mult, OP.add), waits)

        def red(out, in_, op=OP.add, absval=False, waits=()):
            return E("vector", lambda h, o=out, i=in_, p=op, av=absval:
                     h.tensor_reduce(o, i, AX.X, p, apply_absolute_value=av), waits)

        def recip(out, in_, waits=()):
            return E("vector", lambda h, o=out, i=in_: h.reciprocal(o, i), waits)

        def gmemset(ap, val, waits=()):
            return E("gpsimd", lambda h, a=ap, v=val: h.memset(a, v), waits)

        def bcast16(iv_, b=16):
            return AP(iv_.tensor, iv_.offset, list(iv_.ap) + [[0, b]])

        qstate = {"gtok": None}

        def quant(in_v, out_v, fd, w_in, cbuf):
            nb = fd // 16
            cbv = cbuf[:, 0:fd]
            in3 = in_v.rearrange("p (a b) -> p a b", b=16)
            red(bmv[:, 0:nb], in3, OP.max, True, waits=w_in)
            ts(sv[:, 0:nb], bmv[:, 0:nb], 1.0 / 6.0, 1e-12, OP.mult, OP.max)
            recip(iv[:, 0:nb], sv[:, 0:nb])
            xnv = A2[:, 0:fd]
            t3 = tt(xnv.rearrange("p (a b) -> p a b", b=16), in3, bcast16(iv[:, 0:nb]))
            sqv = A3[:, 0:fd]
            ts(sqv, xnv, 1.0 / 2.25, None, OP.mult, waits=[t3])
            E("vector", lambda h, o=cbv.bitcast(mybir.dt.int32),
              i=sqv.bitcast(mybir.dt.int32), m=maskc[:, 0:1]:
              h.tensor_scalar(o, i, m, None, OP.bitwise_and))
            ts(cbv, cbv, 0.5, 12582912.0, OP.max, OP.mult)
            tt(sqv, xnv, cbv, OP.add)
            tt(sqv, sqv, cbv, OP.subtract)
            return tt(out_v.rearrange("p (a b) -> p a b", b=16),
                      sqv.rearrange("p (a b) -> p a b", b=16), bcast16(sv[:, 0:nb]))

        dma(A1[:, 0:256], alog_d.rearrange("p a b -> p (a b)"), qsem="qpa")
        dma(convw[:, :, :], convw_d, qsem="qpa")
        dma(convb[:, :], convb_d, qsem="qpa")
        dma(D_t[:, :], dp_d, qsem="qpa")
        dma(dtb[:, :], dtb_d, qsem="qpa")
        dma(A2[:, 0:1024], gam_d, qsem="qpa")
        t_bt = dma(A3[:, 0:1024], bet_d, qsem="qpa")
        t_al = t_gm = t_bt
        te = act(A_t[:, :], A1[:, 0:256], AF.Exp, waits=[t_al])
        ts(A_t[:, :], A_t[:, :], -1.0, None, OP.mult, waits=[te])
        vcopy(gam[:, :], A2[:, 0:1024], waits=[t_gm])
        vcopy(bet[:, :], A3[:, 0:1024], waits=[t_bt])
        gmemset(ident[:, :], 1.0)
        E("gpsimd", lambda h: h.affine_select(out=ident[:, :], in_=ident[:, :],
                                              pattern=[[-1, 128]], compare_op=OP.is_equal,
                                              fill=0.0, base=0, channel_multiplier=1))
        gmemset(ones1[0:1, :], 1.0)
        gmemset(maskc[:, :], 0x7f800000)
        gmemset(hcar[:, :], 0.0)
        t1 = dma(A1[:, 0:2048], xpw_d.rearrange("p a b -> p (a b)"), qsem="qpb",
                 waits=[("s", pr.cnt["s"]), ("v", pr.cnt["v"]), t_al])
        vcopy(xpw[:, :, :].rearrange("p a b -> p (a b)"), A1[:, 0:2048], waits=[t1])
        t2 = dma(A1[64:128, 0:2048], dtw_d, qsem="qpb", waits=[("v", pr.cnt["v"])])
        vcopy(dtw[64:128, :], A1[64:128, 0:2048], waits=[t2])
        gmemset(sel[:, :], 1.0)
        for pb0 in (0, 32):
            E("gpsimd", lambda h, p0=pb0: h.affine_select(
                out=sel[p0:p0 + 16, :].rearrange("p (a b) -> p a b", a=16),
                in_=sel[p0:p0 + 16, :].rearrange("p (a b) -> p a b", a=16),
                pattern=[[1, 16], [0, 128]],
                compare_op=OP.is_equal, fill=0.0, base=0,
                channel_multiplier=-1))

        with (nc.sbuf_tensor("pbig", [128, 8192], F32) as pbig,
              nc.sbuf_tensor("pbig_b", [128, 8192], BF16) as pbig_b,
              nc.sbuf_tensor("wT", [128, 22528], BF16) as wT):
            for g2 in range(4):
                tg = dma(pbig[:, :], wip_d[:, 2 * g2:2 * g2 + 2, :].rearrange("p a b -> p (a b)"),
                         waits=[("v", pr.cnt["v"])], qsem="qpb")
                tc = vcopy(pbig_b[:, :], pbig[:, :], waits=[tg])
                dma(wip_s[:, 2 * g2:2 * g2 + 2, :].rearrange("p a b -> p (a b)"),
                    pbig_b[:, :], waits=[tc], qsem="qp")
            for g2 in range(2):
                tg = dma(pbig[:, :], wo_d[:, 8 * g2:8 * g2 + 8, :].rearrange("p a b -> p (a b)"),
                         waits=[("v", pr.cnt["v"]), ("qp", pr.cnt["qp"])], qsem="qpb")
                tc = vcopy(pbig_b[:, :], pbig[:, :], waits=[tg])
                dma(wo_s[:, 8 * g2:8 * g2 + 8, :].rearrange("p a b -> p (a b)"),
                    pbig_b[:, :], waits=[tc], qsem="qp")
            for (wsrc, wdst) in ((wgt_d, wgq_s), (wut_d, wuq_s)):
                for ic in range(22):
                    tg = dma(pbig[:, 0:1024], wsrc[:, ic, :],
                             waits=[("v", pr.cnt["v"]), ("t", pr.cnt["t"]),
                                    ("qp", pr.cnt["qp"])], qsem="qpb")
                    tq = quant(pbig[:, 0:1024], pbig_b[:, 0:1024], 1024, [tg], cbuf=A1)
                    pb = pt_b[ic % 2]
                    wts = fence_ps(("pt", ic % 2)) + [tq]
                    for kc in range(8):
                        trans(pb[:, kc, :], pbig_b[:, kc * 128:(kc + 1) * 128],
                              waits=wts)
                        wts = []
                    ev = scopy(wT.rearrange("p (a b) -> p a b", a=8)[:, :, ic * 128:(ic + 1) * 128],
                               pb[:, :, :], waits=[("t", pr.cnt["t"])])
                    prd[("pt", ic % 2)] = ev
                dma(wdst[:, :, :].rearrange("p a b -> p (a b)"), wT[:, :],
                    waits=[("s", pr.cnt["s"])], qsem="qp")
            for hc in range(8):
                tg = dma(pbig[:, 0:2816], wdt_d[:, hc, :],
                         waits=[("v", pr.cnt["v"]), ("t", pr.cnt["t"]),
                                ("qp", pr.cnt["qp"])], qsem="qpb")
                tq0 = quant(pbig[:, 0:1408], pbig_b[:, 0:1408], 1408, [tg], cbuf=A1)
                tq = quant(pbig[:, 1408:2816], pbig_b[:, 1408:2816], 1408, [tq0], cbuf=A1)
                for ic in range(22):
                    pb = pt_b[ic % 2]
                    wts = fence_ps(("pt", ic % 2)) + [tq]
                    trans(pb[:, 0, :], pbig_b[:, ic * 128:(ic + 1) * 128], waits=wts)
                    ev = scopy(wT.rearrange("p (a b) -> p a b", a=22)[:, ic, hc * 128:(hc + 1) * 128],
                               pb[:, 0, :], waits=[("t", pr.cnt["t"])])
                    prd[("pt", ic % 2)] = ev
            dma(wdq_s[:, :, :].rearrange("p a b -> p (a b)"), wT[:, :],
                waits=[("s", pr.cnt["s"])], qsem="qp")
            pr.join()

        xres = sb("xres_t", [128, NTI, 1024], F32)
        xq = sb("xq_t", [128, NTI, 1024], BF16)
        xqT = sb("xqT_t", [128, 8, NTI * 128], BF16)
        h_f = sb("h_f", [128, NTI, 2816], F32)
        hq_ti2 = sb("hq_ti", [128, NTI, 2816], BF16)
        hqT_all = sb("hqT_t", [128, 22, NTI * 128], BF16)
        sgb = sb("sgb_t", [128, 512], BF16)
        lnm = sb("lnm_t", [128, NTI, 1024], BF16)
        lnmT = sb("lnmT_t", [128, 8, TC], BF16)
        xcv = sb("xcv_t", [128, 16, TC + 3], BF16)
        zsl = sb("zsl_t", [128, 16, TC], BF16)
        u16 = sb("u16_t", [128, 16, TC], BF16)
        hbuf = sb("hbuf_t", [128, 16, TC], BF16)
        yg = sb("yg_t", [128, 16, TC], BF16)
        Bbc = sb("Bbc_t", [128, 16, TC], BF16)
        Cbc = sb("Cbc_t", [128, 16, TC], BF16)
        dtr = sb("dtr_t", [128, TC], BF16)
        bcsb = sb("bcsb_t", [48, 2 * TC], BF16)
        delta2 = sb("delta2_t", [128, 2, TC], F32)
        du2 = sb("du2_t", [128, 2, TC], BF16)
        accA = sb("accA_t", [128, TC], F32)
        accB = sb("accB_t", [128, TC], F32)
        yv = sb("yv_t", [128, TC], F32)
        y2v = sb("y2v_t", [128, TC], F32)
        wreg = sb("wreg_t", [128, 16384], BF16)

        gmemset(xcv[:, :, :].rearrange("p a b -> p (a b)"), 0.0)
        pr.join()

        wranges = []

        def wfence(off, size):
            w = [t for (o, sz, t) in wranges if o < off + size and off < o + sz]
            return w

        def wmark(off, size, tok):
            wranges[:] = [(o, sz, t) for (o, sz, t) in wranges
                          if not (o >= off and o + sz <= off + size)]
            wranges.append((off, size, tok))

        def wview(off, n_a, n_b):
            return wreg[:, off:off + n_a * n_b].rearrange("p (a b) -> p a b", a=n_a)

        def ln_stats_group(src, w_in):
            w = list(w_in)
            for ti in range(NTI):
                E("vector", lambda h, i=src[:, ti, 0:512], o=bnb[:, ti * 12:ti * 12 + 6]:
                  h.bn_stats(o, i), waits=w + [("s", pr.cnt["s"])])
                w = []
                tl = E("vector", lambda h, i=src[:, ti, 512:1024],
                       o=bnb[:, ti * 12 + 6:ti * 12 + 12]: h.bn_stats(o, i))
            pr.emit("vector", lambda h: h.nop(), waits=[tl])
            for ti in range(NTI):
                ta = E("vector", lambda h, o=stat[:, ti, 0:2], i=bnb[:, ti * 12:ti * 12 + 12]:
                       h.bn_aggr(o, i))
            pr.emit("vector", lambda h: h.nop(), waits=[ta])
            vvar = stat[:, :, 1]
            vr = stat[:, :, 2]
            tv = ts(vr, vvar, 1e-5, None, OP.add, waits=[("s", pr.cnt["s"])])
            tl = act(vr, vr, AF.Ln, waits=[tv])
            return act(vr, vr, AF.Exp, scale=-0.5, waits=[tl])

        def ln_apply(ti, xin, out_v, w=()):
            return ts(out_v, xin, stat[:, ti, 0:1], stat[:, ti, 2:3],
                      OP.subtract, OP.mult, waits=w)

        I_GROUPS = [(0, 512), (512, 512), (1024, 512), (1536, 512),
                    (2048, 512), (2560, 256)]
        H_SL = [(i * 256, 256) for i in range(4)]

        def ffn_stage(tx):
            tst = ln_stats_group(xres, [tx])
            tqx = {}
            for ti in range(NTI):
                ta = ln_apply(ti, xres[:, ti, :], A1[:, 0:1024], [tst])
                tqx[ti] = quant(A1[:, 0:1024], xq[:, ti, :], 1024, [ta],
                                cbuf=h_f[:, ti, :])
            for ti in range(NTI):
                pb = pt_b[ti % 2]
                wts = fence_ps(("pt", ti % 2)) + [tqx[ti]]
                for dc in range(8):
                    trans(pb[:, dc, :], xq[:, ti, dc * 128:(dc + 1) * 128], waits=wts)
                    wts = []
                ev = scopy(xqT[:, :, ti * 128:(ti + 1) * 128], pb[:, :, :],
                           waits=[("t", pr.cnt["t"])])
                prd[("pt", ti % 2)] = ev
            txqT = ("s", pr.cnt["s"])
            gtok = {}
            for gi in range(2):
                i0, iw = I_GROUPS[gi]
                go, uo = (gi % 2) * 4096, 8192 + (gi % 2) * 4096
                gtok[("g", gi)] = (go, dma(wview(go, 8, iw), wgq_s[:, :, i0:i0 + iw],
                                           waits=wfence(go, 8 * iw),
                                           qsem="qg%d" % (gi % 2)))
                gtok[("u", gi)] = (uo, dma(wview(uo, 8, iw), wuq_s[:, :, i0:i0 + iw],
                                           waits=wfence(uo, 8 * iw),
                                           qsem="qu%d" % (gi % 2)))
            thp = {"tok": None}

            def gateup_group(gi):
                i0, iw = I_GROUPS[gi]
                go, twg = gtok[("g", gi)]
                uo, twu = gtok[("u", gi)]
                for ti in range(NTI):
                    pg, pu = pmm[0], pmm[1]
                    wts = fence_ps(("pmm", 0)) + [twg, txqT]
                    for dc in range(8):
                        mm(pg[:, 0:iw], xqT[:, dc, ti * 128:(ti + 1) * 128],
                           wview(go, 8, iw)[:, dc, :], dc == 0, dc == 7, waits=wts)
                        wts = []
                    tg = ("t", pr.cnt["t"])
                    wts = fence_ps(("pmm", 1)) + [twu]
                    for dc in range(8):
                        mm(pu[:, 0:iw], xqT[:, dc, ti * 128:(ti + 1) * 128],
                           wview(uo, 8, iw)[:, dc, :], dc == 0, dc == 7, waits=wts)
                        wts = []
                    tu = ("t", pr.cnt["t"])
                    tsg = act(sgb[:, 0:iw], pg[:, 0:iw], AF.Silu,
                              waits=[tg, thp["tok"]])
                    prd[("pmm", 0)] = tsg
                    th = tt(h_f[:, ti, i0:i0 + iw], sgb[:, 0:iw],
                            pu[:, 0:iw], OP.mult, waits=[tu, tsg])
                    prd[("pmm", 1)] = th
                    thp["tok"] = th
                tmk = ("t", pr.cnt["t"])
                wmark(go, 8 * iw, tmk)
                wmark(uo, 8 * iw, tmk)
                if gi + 2 < len(I_GROUPS):
                    i2, iw2 = I_GROUPS[gi + 2]
                    go2, uo2 = (gi % 2) * 4096, 8192 + (gi % 2) * 4096
                    gtok[("g", gi + 2)] = (go2, dma(wview(go2, 8, iw2),
                                                    wgq_s[:, :, i2:i2 + iw2],
                                                    waits=wfence(go2, 8 * iw2),
                                                    qsem="qg%d" % (gi % 2)))
                    gtok[("u", gi + 2)] = (uo2, dma(wview(uo2, 8, iw2),
                                                    wuq_s[:, :, i2:i2 + iw2],
                                                    waits=wfence(uo2, 8 * iw2),
                                                    qsem="qu%d" % (gi % 2)))
            for gi in (0, 1, 2):
                gateup_group(gi)
            tq_half = {}
            for ti in range(NTI):
                tq_half[(ti, 0)] = quant(h_f[:, ti, 0:1408], hq_ti2[:, ti, 0:1408],
                                         1408, [], cbuf=A1)
            for gi in (3, 4, 5):
                gateup_group(gi)
            for ti in range(NTI):
                tq_half[(ti, 1)] = quant(h_f[:, ti, 1408:2816],
                                         hq_ti2[:, ti, 1408:2816], 1408, [], cbuf=A1)
            for ti in range(NTI):
                tqh = tq_half[(ti, 1)]
                for g3 in range(3):
                    ics = list(range(8 * g3, min(22, 8 * g3 + 8)))
                    pb = pt_b[g3 % 2]
                    wts = fence_ps(("pt", g3 % 2)) + [tq_half[(ti, 0)], tqh]
                    for j, ic in enumerate(ics):
                        trans(pb[:, j, :], hq_ti2[:, ti, ic * 128:(ic + 1) * 128],
                              waits=wts)
                        wts = []
                    ev = scopy(hqT_all[:, ics[0]:ics[-1] + 1, ti * 128:(ti + 1) * 128],
                               pb[:, 0:len(ics), :], waits=[("t", pr.cnt["t"])])
                    prd[("pt", g3 % 2)] = ev
            tT = ("s", pr.cnt["s"])
            wd_tok = {}
            for si in range(2):
                o0, ow = H_SL[si]
                off = (si % 2) * 5632
                wd_tok[si] = dma(wview(off, 22, ow), wdq_s[:, :, o0:o0 + ow],
                                 waits=wfence(off, 22 * ow), qsem="qd%d" % (si % 2))
            for si, (o0, ow) in enumerate(H_SL):
                off = (si % 2) * 5632
                twd = wd_tok[si]
                for ti in range(NTI):
                    pd = pmm[2 + si % 2]
                    wts = fence_ps(("pmm", 2 + si % 2)) + [twd, tT]
                    for ic in range(22):
                        mm(pd[:, 0:ow], hqT_all[:, ic, ti * 128:(ti + 1) * 128],
                           wview(off, 22, ow)[:, ic, :], ic == 0, ic == 21,
                           waits=wts)
                        wts = []
                    tre = stt(xres[:, ti, o0:o0 + ow], pd[:, 0:ow], 0.5,
                              xres[:, ti, o0:o0 + ow], OP.mult, OP.add,
                              waits=[("t", pr.cnt["t"])])
                    prd[("pmm", 2 + si % 2)] = tre
                wmark(off, 22 * ow, ("t", pr.cnt["t"]))
                if si + 2 < len(H_SL):
                    o2, ow2 = H_SL[si + 2]
                    wd_tok[si + 2] = dma(wview(off, 22, ow2),
                                         wdq_s[:, :, o2:o2 + ow2],
                                         waits=wfence(off, 22 * ow2),
                                         qsem="qd%d" % (si % 2))

        def mamba_stage(tx, skip_out=False):
            scr = h_f
            tst = ln_stats_group(xres, [tx])
            for ti in range(NTI):
                ln_apply(ti, xres[:, ti, :], scr[:, ti, 0:1024], [tst])
            tst2 = ln_stats_group(scr[:, :, 0:1024], [("v", pr.cnt["v"])])
            for ti in range(NTI):
                tb = ln_apply(ti, scr[:, ti, 0:1024], A2[:, 0:1024], [tst2])
                tc = tt(A3[:, 0:1024], A2[:, 0:1024], gam[:, :], OP.mult, waits=[tb])
                tt(lnm[:, ti, :], A3[:, 0:1024], bet[:, :], OP.add, waits=[tc])
            for ti in range(NTI):
                pb = pt_b[ti % 2]
                wts = fence_ps(("pt", ti % 2)) + [("v", pr.cnt["v"])]
                for kc in range(8):
                    trans(pb[:, kc, :], lnm[:, ti, kc * 128:(kc + 1) * 128], waits=wts)
                    wts = []
                ev = scopy(lnmT[:, :, ti * 128:(ti + 1) * 128], pb[:, :, :],
                           waits=[("t", pr.cnt["t"])])
                prd[("pt", ti % 2)] = ev
            if MAMBA_CUT == 1:
                return
            tT = ("s", pr.cnt["s"])
            wip_tok = {}
            for sl in range(2):
                off = (sl % 2) * 8192
                wip_tok[sl] = dma(wview(off, 8, 1024),
                                  wip_s[:, :, sl * 1024:(sl + 1) * 1024],
                                  waits=wfence(off, 8192), qsem="qi%d" % (sl % 2))
            for sl in range(4):
                off = (sl % 2) * 8192
                twip = wip_tok[sl]
                for mc8 in range(8):
                    m = sl * 8 + mc8
                    pd = pmm[m % 2]
                    wts = fence_ps(("pmm", m % 2)) + [twip, tT]
                    for kc in range(8):
                        mm(pd[:, 0:TC], wview(off, 8, 1024)[:, kc, mc8 * 128:(mc8 + 1) * 128],
                           lnmT[:, kc, :], kc == 0, kc == 7, waits=wts)
                        wts = []
                    tmv = ("t", pr.cnt["t"])
                    if m < 16:
                        ev = scopy(xcv[:, m, 3:TC + 3], pd[:, 0:TC], waits=[tmv])
                    else:
                        ev = act(zsl[:, m - 16, :], pd[:, 0:TC], AF.Silu, waits=[tmv])
                    prd[("pmm", m % 2)] = ev
                wmark(off, 8192, ("t", pr.cnt["t"]))
                if sl + 2 < 4:
                    wip_tok[sl + 2] = dma(wview(off, 8, 1024),
                                          wip_s[:, :, (sl + 2) * 1024:(sl + 3) * 1024],
                                          waits=wfence(off, 8192),
                                          qsem="qi%d" % (sl % 2))
            tz = ("s", pr.cnt["s"])
            for dc in range(16):
                w_in = [tz] if dc == 0 else []
                ts(accA[:, :], xcv[:, dc, 0:TC], convw[:, dc, 0:1], None,
                   OP.mult, OP.bypass, waits=w_in)
                stt(accB[:, :], xcv[:, dc, 1:TC + 1], convw[:, dc, 1:2], accA[:, :])
                stt(accA[:, :], xcv[:, dc, 2:TC + 2], convw[:, dc, 2:3], accB[:, :])
                t3 = stt(accB[:, :], xcv[:, dc, 3:TC + 3], convw[:, dc, 3:4], accA[:, :])
                act(u16[:, dc, :], accB[:, :], AF.Silu, bias=convb[:, dc:dc + 1],
                    waits=[t3])
                vcopy(xcv[:, dc, 0:3], xcv[:, dc, TC:TC + 3])
            if MAMBA_CUT == 2:
                return
            tu_all = ("s", pr.cnt["s"])
            pxp = pmm[2]
            wts = fence_ps(("pmm", 2)) + [tu_all]
            for kc in range(16):
                mm(pxp[:, 0:TC], xpw[:, kc, :], u16[:, kc, :], kc == 0, kc == 15,
                   waits=wts)
                wts = []
            txp = ("t", pr.cnt["t"])
            e1 = scopy(dtr[64:128, :], pxp[64:128, 0:TC], waits=[txp])
            scopy(bcsb[0:16, 0:TC], pxp[0:16, 0:TC])
            e3 = scopy(bcsb[32:48, 0:TC], pxp[32:48, 0:TC])
            prd[("pmm", 2)] = e3
            for n in range(16):
                for which in range(2):
                    bk = 2 + ((n * 2 + which) % 2)
                    pb2 = pmm[bk]
                    wts = fence_ps(("pmm", bk)) + [e3]
                    p0 = 0 if which == 0 else 32
                    mm(pb2[:, 0:TC], sel[p0:p0 + 16, n * 128:(n + 1) * 128],
                       bcsb[p0:p0 + 16, 0:TC], True, True, waits=wts)
                    dst = Bbc if which == 0 else Cbc
                    ev = scopy(dst[:, n, :], pb2[:, 0:TC], waits=[("t", pr.cnt["t"])])
                    prd[("pmm", bk)] = ev
            if MAMBA_CUT == 3:
                return
            tbc = ("s", pr.cnt["s"])
            par_tok = {0: [], 1: []}
            scan_tok = {k: None for k in range(8)}
            for dc in range(16):
                db = dc % 2
                pd = pmm[dc % 4]
                wts = fence_ps(("pmm", dc % 4)) + ([e1] if dc == 0 else []) + par_tok[db]
                mm(pd[:, 0:TC], dtw[64:128, dc * 128:(dc + 1) * 128], dtr[64:128, :],
                   True, True, waits=wts)
                tdt = ("t", pr.cnt["t"])
                dl = delta2[:, db, :]
                duv = du2[:, db, :]
                tm0 = ts(dl, pd[:, 0:TC], dtb[:, dc:dc + 1], 40.0, OP.add, OP.min,
                         waits=[tdt] + par_tok[db])
                tm1 = act(dl, dl, AF.Exp, waits=[tm0])
                tm2 = act(dl, dl, AF.Ln, bias=1.0, waits=[tm1])
                tm3 = stt(dl, pd[:, 0:TC], dtb[:, dc:dc + 1], dl, OP.add, OP.max,
                          waits=[tm2])
                prd[("pmm", dc % 4)] = tm3
                tdu = tt(duv, dl, u16[:, dc, :], OP.mult, waits=[tm3])
                dball = h_f[:, 0, 0:2048].bitcast(BF16).rearrange(
                    "p (a b) -> p a b", a=16)
                dA8 = h_f[:, 1, 0:2048].rearrange("p (a b) -> p a b", a=8)
                duv_b = AP(duv.tensor, duv.offset,
                           [list(duv.ap[0]), [0, 16], list(duv.ap[1])])
                tt(dball, duv_b, Bbc[:, :, :], OP.mult, waits=[tdu, tbc])
                for n in range(16):
                    nb = n % 8
                    tda = act(dA8[:, nb, :], dl, AF.Exp,
                              scale=A_t[:, dc * 16 + n:dc * 16 + n + 1],
                              waits=[scan_tok[nb], tm3])
                    tsc = scan_op(hbuf[:, n, :], dA8[:, nb, :], dball[:, n, :],
                                  hcar[:, dc * 16 + n:dc * 16 + n + 1],
                                  waits=[tda])
                    scan_tok[nb] = tsc
                tcar = vcopy(hcar[:, dc * 16:dc * 16 + 16],
                             hbuf[:, :, TC - 1:TC].rearrange("p a b -> p (a b)"))
                tpr = tt(hbuf[:, :, :].rearrange("p a b -> p (a b)"),
                         hbuf[:, :, :].rearrange("p a b -> p (a b)"),
                         Cbc[:, :, :].rearrange("p a b -> p (a b)"), OP.mult,
                         waits=[tcar])
                tyr = red(yv[:, :], hbuf[:, :, :].transpose([0, 2, 1]), OP.add,
                          waits=[tpr])
                for _k in range(8):
                    scan_tok[_k] = tyr
                ty2 = stt(y2v[:, :], u16[:, dc, :], D_t[:, dc:dc + 1], yv[:, :],
                          OP.mult, OP.add, waits=[tyr])
                tt(yg[:, dc, :], y2v[:, :], zsl[:, dc, :], OP.mult, waits=[ty2])
                par_tok[db] = [("s", pr.cnt["s"]), ("v", pr.cnt["v"]),
                               ("g", pr.cnt["g"])]
            if MAMBA_CUT == 4 or skip_out:
                return
            tyg_all = ("v", pr.cnt["v"])
            wo_tok = {}
            for nc_i in range(2):
                off = nc_i * 8192
                wo_tok[nc_i] = dma(wview(off, 16, 512),
                                   wo_s[:, :, nc_i * 512:(nc_i + 1) * 512],
                                   waits=wfence(off, 8192), qsem="qw%d" % (nc_i % 2))
            for nc_i in range(2):
                off = nc_i * 8192
                two = wo_tok[nc_i]
                for ti in range(NTI):
                    pd = pmm[ti % 2]
                    wts = fence_ps(("pmm", ti % 2)) + [two, tyg_all]
                    for kc in range(16):
                        mm(pd[:, :], yg[:, kc, ti * 128:(ti + 1) * 128],
                           wview(off, 16, 512)[:, kc, :], kc == 0, kc == 15, waits=wts)
                        wts = []
                    tre = tt(xres[:, ti, nc_i * 512:(nc_i + 1) * 512],
                             xres[:, ti, nc_i * 512:(nc_i + 1) * 512], pd[:, :],
                             OP.add, waits=[("t", pr.cnt["t"])])
                    prd[("pmm", ti % 2)] = tre
                wmark(off, 8192, ("t", pr.cnt["t"]))

        dbg = {}
        if DEV_DEBUG:
            dbg["xq"] = nc.dram_tensor("dbg_xq", [128, NTI, 1024], BF16,
                                       kind="ExternalOutput").ap()
            dbg["hf"] = nc.dram_tensor("dbg_hf", [128, NTI, 2816], F32,
                                       kind="ExternalOutput").ap()
            dbg["wg"] = nc.dram_tensor("dbg_wg", [128, 8, 2816], BF16,
                                       kind="ExternalOutput").ap()
            dbg["xqT"] = nc.dram_tensor("dbg_xqT", [128, 8, NTI * 128], BF16,
                                        kind="ExternalOutput").ap()
            dbg["ln"] = nc.dram_tensor("dbg_ln", [128, 1024], F32,
                                       kind="ExternalOutput").ap()
            dbg["stat"] = nc.dram_tensor("dbg_stat", [128, NTI, 4], F32,
                                         kind="ExternalOutput").ap()
            dbg["sv"] = nc.dram_tensor("dbg_sv", [128, 88], F32,
                                       kind="ExternalOutput").ap()
            dbg["Bbc"] = nc.dram_tensor("dbg_Bbc", [128, 16, TC], BF16,
                                        kind="ExternalOutput").ap()
            dbg["Cbc"] = nc.dram_tensor("dbg_Cbc", [128, 16, TC], BF16,
                                        kind="ExternalOutput").ap()
            dbg["u16"] = nc.dram_tensor("dbg_u16", [128, 16, TC], BF16,
                                        kind="ExternalOutput").ap()
            dbg["zsl"] = nc.dram_tensor("dbg_zsl", [128, 16, TC], BF16,
                                        kind="ExternalOutput").ap()
            dbg["dtr"] = nc.dram_tensor("dbg_dtr", [128, TC], BF16,
                                        kind="ExternalOutput").ap()
            dbg["yg"] = nc.dram_tensor("dbg_yg", [128, 16, TC], BF16,
                                       kind="ExternalOutput").ap()
            dbg["hb"] = nc.dram_tensor("dbg_hb", [128, 16, TC], BF16,
                                       kind="ExternalOutput").ap()
        for c in range(DEV_NCH or NCH):
            tx = dma(xres[:, :, :],
                     x_d[c * TC:(c + 1) * TC, :].rearrange("(a p) d -> p a d", p=128),
                     qsem="qx", waits=[("qy", pr.cnt["qy"]), ("v", pr.cnt["v"])])
            if DEV_STAGES >= 1 and DEV_STAGES != 5:
                ffn_stage(tx)
            else:
                E("vector", lambda h: h.nop(), waits=[tx])
            if DEV_DEBUG and c == 0:
                w_all = [(s_, pr.cnt[s_]) for s_ in ("v", "s", "t")]
                dma(dbg["xq"], xq[:, :, :], waits=w_all, qsem="qo")
                dma(dbg["ln"], A1[:, 0:1024], waits=w_all, qsem="qo")
                dma(dbg["stat"], stat[:, :, :], waits=w_all, qsem="qo")
                dma(dbg["sv"], sv[:, :], waits=w_all, qsem="qo")
                dma(dbg["hf"], h_f[:, :, :], waits=w_all, qsem="qo")
                dma(dbg["xqT"], xqT[:, :, :], waits=w_all, qsem="qo")
                tb = dma(wview(0, 8, 1024), wgq_s[:, :, 0:1024], waits=fence_ps(("wr", 0)))
                dma(dbg["wg"][:, :, 0:1024], wview(0, 8, 1024), waits=[tb], qsem="qo")
                prd[("wr", 0)] = ("t", pr.cnt["t"])
            if DEV_STAGES >= 2 or DEV_STAGES == 5:
                mamba_stage(tx, skip_out=(c in MAMBA_SKIP_OUT))
                if DEV_DEBUG and c == 0:
                    w_all = [(s_, pr.cnt[s_]) for s_ in ("v", "s", "t", "g")]
                    for k_, b_ in (("Bbc", Bbc), ("Cbc", Cbc), ("u16", u16),
                                   ("zsl", zsl), ("yg", yg), ("hb", hbuf)):
                        if k_ in dbg:
                            dma(dbg[k_], b_[:, :, :], waits=w_all, qsem="qo")
                    dma(dbg["dtr"], dtr[:, :], waits=w_all, qsem="qo")
            if DEV_STAGES >= 3:
                ffn_stage(tx)
            dma(y_d[c * TC:(c + 1) * TC, :].rearrange("(a p) d -> p a d", p=128),
                xres[:, :, :],
                waits=[("v", pr.cnt["v"]), ("s", pr.cnt["s"]), ("t", pr.cnt["t"])],
                qsem="qy")
        pr.emit("sync", lambda h: h.nop(), waits=[("qy", pr.cnt["qy"])])
        with nc.Block() as block:
            def runner(eng):
                def go(h):
                    for ws, fn, inc, k in pr.q[eng]:
                        for s_, v_ in ws:
                            h.wait_ge(sem[s_], v_)
                        op = fn(h)
                        if inc is not None:
                            op.then_inc(sem[inc], k)
                return go
            block.sync(runner("sync"))
            block.scalar(runner("scalar"))
            block.vector(runner("vector"))
            block.tensor(runner("tensor"))
            block.gpsimd(runner("gpsimd"))
    return nc


def _host_layout(inp):
    f = np.float32
    g = lambda k: np.asarray(inp[k], dtype=f)
    d = {}
    d["wip"] = np.ascontiguousarray(g("L0_attn_in_proj_w").reshape(8, 128, 4096).transpose(1, 0, 2))
    d["wo"] = np.ascontiguousarray(g("L0_attn_out_proj_w").reshape(16, 128, 1024).transpose(1, 0, 2))
    xpw_r = g("L0_attn_x_proj_w")
    z16 = np.zeros((2048, 16), np.float32)
    xpw_r = np.concatenate([xpw_r[:, 64:80], z16, xpw_r[:, 80:96], z16,
                            xpw_r[:, 0:64]], axis=1)
    d["xpw"] = np.ascontiguousarray(xpw_r.reshape(16, 128, 128).transpose(1, 0, 2))
    d["dtw"] = np.ascontiguousarray(g("L0_attn_dt_proj_w"))
    d["wgt"] = np.ascontiguousarray(g("L0_ffn_w_gate").reshape(22, 128, 1024).transpose(1, 0, 2))
    d["wut"] = np.ascontiguousarray(g("L0_ffn_w_up").reshape(22, 128, 1024).transpose(1, 0, 2))
    d["wdt"] = np.ascontiguousarray(g("L0_ffn_w_down").reshape(8, 128, 2816).transpose(1, 0, 2))
    d["convw"] = np.ascontiguousarray(g("L0_attn_conv_w").reshape(16, 128, 4).transpose(1, 0, 2))
    d["convb"] = np.ascontiguousarray(g("L0_attn_conv_b").reshape(16, 128).T)
    d["alog"] = np.ascontiguousarray(g("L0_attn_A_log").reshape(16, 128, 16).transpose(1, 0, 2))
    d["dp"] = np.ascontiguousarray(g("L0_attn_D_param").reshape(16, 128).T)
    d["dtb"] = np.ascontiguousarray(g("L0_attn_dt_proj_b").reshape(16, 128).T)
    d["gam"] = np.ascontiguousarray(np.broadcast_to(g("L0_attn_norm_w"), (128, 1024)))
    d["bet"] = np.ascontiguousarray(np.broadcast_to(g("L0_attn_norm_b"), (128, 1024)))
    return d


def kernel(**inputs):
    hs = np.ascontiguousarray(np.asarray(inputs["L0_attn_hidden_states"], dtype=np.float32))
    shared = _host_layout(inputs)
    nc = build_nc()
    in_maps = [dict(shared, x=np.ascontiguousarray(hs[c])) for c in range(8)]
    res = run_bass_kernel_spmd(nc, in_maps, core_ids=list(range(8)))
    out = np.stack([np.asarray(res.results[c]["y"]) for c in range(8)], axis=0)
    return out.astype(np.float32)
